# version 02
# baseline (speedup 1.0000x reference)
import contextlib
import numpy as np
import ml_dtypes
import concourse.bass as bass
import concourse.mybir as mybir
from concourse.bass_utils import run_bass_kernel_spmd

F32 = mybir.dt.float32
BF16 = mybir.dt.bfloat16
I32 = mybir.dt.int32
AF = mybir.ActivationFunctionType
ALU = mybir.AluOpType
AXX = mybir.AxisListType.X

NCORES = 8
DMODEL = 4096
SEQ = 8192
T = SEQ // NCORES
NT = T // 128
NI = DMODEL // 128
H, KVH, D, CHUNK = 32, 8, 128, 128
NCH = T // CHUNK
LN_EPS = 1e-5
RMS_EPS = 1e-5
BF = ml_dtypes.bfloat16
_UID = [0]


def _stage_pfx():
    _UID[0] += 1
    return "s%d_" % _UID[0]


class Sched:

    ENGS = ("te", "vec", "sc", "gp", "sy")

    def __init__(self):
        self.ops = {k: [] for k in self.ENGS}
        self.count = {}

    def tok(self, sem):
        return (sem, self.count.get(sem, 0))

    def add(self, eng, fn, waits=(), inc=None, inc_n=1):
        if inc is not None:
            self.count[inc] = self.count.get(inc, 0) + inc_n
            tok = (inc, self.count[inc])
        else:
            tok = None
        self.ops[eng].append((fn, tuple(w for w in waits if w is not None),
                              inc, inc_n))
        return tok

    def dma(self, eng, fn, waits=()):
        return self.add(eng, fn, waits=waits, inc=eng + "_d", inc_n=16)

    def barrier_wait(self, eng, tok):
        self.add(eng, lambda e: e.nop(), waits=[tok])

    def barrier(self):
        t1, t2 = self.tok("sy_d"), self.tok("gp_d")
        for e in self.ENGS:
            self.add(e, lambda e_: e_.nop(), waits=[t1, t2], inc="bar")
        bt = self.tok("bar")
        for e in self.ENGS:
            self.add(e, lambda e_: e_.nop(), waits=[bt])

    def emit(self, nc):
        names = set()
        for lst in self.ops.values():
            for _, waits, inc, _ in lst:
                names.update(s0 for s0, _ in waits)
                if inc:
                    names.add(inc)
        with contextlib.ExitStack() as stack:
            sems = {n: stack.enter_context(nc.semaphore(n))
                    for n in sorted(names)}
            with nc.Block() as block:
                def mk(eng_key):
                    def _(eng):
                        for fn, waits, inc, inc_n in self.ops[eng_key]:
                            for (s0, v) in waits:
                                if v > 0:
                                    eng.wait_ge(sems[s0], v)
                            inst = fn(eng)
                            if inc is not None:
                                assert inst is not None, eng_key
                                inst.then_inc(sems[inc], inc_n)
                    return _
                block.tensor(mk("te"))
                block.vector(mk("vec"))
                block.scalar(mk("sc"))
                block.gpsimd(mk("gp"))
                block.sync(mk("sy"))


def emit_ln_to_xnT(s, bufs, x_hbm, pfx="ln"):
    xtile, stats, xnbf, xnT, identb, ptr, id_tok = bufs
    ssum = stats[:, 0:1]
    ssq = stats[:, 1:2]
    nmean = stats[:, 2:3]
    var = stats[:, 3:4]
    rstd = stats[:, 4:5]
    nbx = 1 if xtile[0] is xtile[1] else 2
    nbn = 1 if xnbf[0] is xnbf[1] else 2
    prev_users = [None, None]
    prev_sq_rd = None
    prev_rstd_rd = None
    prev_var_rd = None
    prev_tr = [None, None]
    evacs = []
    last = None
    for t in range(NT):
        p = t % nbx
        xt = xtile[p]
        ld = s.dma("sy", lambda e, xt=xt, t=t: e.dma_start(
            out=xt[:, :], in_=x_hbm[t * 128:(t + 1) * 128, :]),
            waits=[prev_users[p]])
        sm = s.add("vec", lambda e, xt=xt: e.tensor_reduce(
            ssum[:, :], xt[:, :], AXX, ALU.add), waits=[ld], inc=pfx + "v")
        qq = s.add("sc", lambda e, xt=xt, pn=t % nbn: e.activation(
            xnbf[pn][:, :], xt[:, :], AF.Square, accum_out=ssq[:, :]),
            waits=[ld, prev_sq_rd, prev_tr[t % nbn]], inc=pfx + "s")
        nm = s.add("vec", lambda e: e.tensor_scalar(
            nmean[:, :], ssum[:, :], -1.0 / DMODEL, None, ALU.mult),
            waits=[sm], inc=pfx + "v")
        m2 = s.add("vec", lambda e: e.tensor_tensor(
            var[:, :], nmean[:, :], nmean[:, :], ALU.mult),
            waits=[nm, prev_var_rd], inc=pfx + "v")
        vv = s.add("vec", lambda e: e.scalar_tensor_tensor(
            var[:, :], ssq[:, :], 1.0 / DMODEL, var[:, :], ALU.mult,
            ALU.subtract), waits=[m2, qq], inc=pfx + "v")
        prev_sq_rd = vv
        l1 = s.add("sc", lambda e: e.activation(
            rstd[:, :], var[:, :], AF.Ln, bias=np.float32(LN_EPS)),
            waits=[vv, prev_rstd_rd], inc=pfx + "s")
        prev_var_rd = l1
        l2 = s.add("sc", lambda e: e.activation(
            rstd[:, :], rstd[:, :], AF.Exp, scale=np.float32(-0.5)), waits=[l1],
            inc=pfx + "s")
        pn = t % nbn
        nrm = s.add("vec", lambda e, xt=xt, pn=pn: e.tensor_scalar(
            xnbf[pn][:, :], xt[:, :], nmean[:, :], rstd[:, :], ALU.add,
            ALU.mult), waits=[l2, prev_tr[pn]], inc=pfx + "v")
        prev_rstd_rd = nrm
        for i in range(NI):
            n = t * NI + i
            pb = n % 2
            tr = s.add("te", lambda e, i=i, pn=pn, pb=pb: e.transpose(
                ptr[pb][:, 0:128], xnbf[pn][:, i * 128:(i + 1) * 128],
                identb[:, :]),
                waits=[nrm, id_tok, evacs[n - 2] if n >= 2 else None],
                inc=pfx + "t")
            ev = s.add("vec", lambda e, i=i, t=t, pb=pb: e.tensor_copy(
                xnT[:, i * T + t * 128: i * T + (t + 1) * 128],
                ptr[pb][:, 0:128]), waits=[tr], inc=pfx + "e")
            evacs.append(ev)
            last = ev
        prev_users[p] = s.tok(pfx + "t")
        prev_tr[t % nbn] = s.tok(pfx + "t")
    return last


def emit_ffn(s, nc, io, kind):
    x = io["x"]
    ident = io["ident"]
    bias = io["bias"]
    ones_r = io["ones_r"]
    wT = io["wT"]
    if kind == "ffn0":
        scinv = io["scinv"]
    elif kind == "ffn1":
        qscale = io["qscale"]
        loraAT = io["loraAT"]
        loraBT = io["loraBT"]
    xo = io["xo"]

    half = 0.5 if kind != "maskmm" else 1.0

    with contextlib.ExitStack() as stk:
        _p = _stage_pfx()

        def sb(name, shape, dt):
            return stk.enter_context(nc.sbuf_tensor(_p + name, shape, dt))

        def ps(name, shape, dt):
            return stk.enter_context(nc.psum_tensor(_p + name, shape, dt))

        xtile0 = sb("xtile0", [128, DMODEL], F32)
        xtile1 = sb("xtile1", [128, DMODEL], F32)
        stats = sb("stats", [128, 8], F32)
        xnbf0 = sb("xnbf0", [128, DMODEL], BF16)
        xnbf1 = sb("xnbf1", [128, DMODEL], BF16)
        xnT = sb("xnT", [128, NI * T], BF16)
        identb = sb("identb", [128, 128], BF16)
        slab = sb("slab", [128, 4 * 4096], BF16)
        biasb = sb("biasb", [1, DMODEL], BF16)
        onesb = sb("onesb", [1, 128], BF16)
        onesf = sb("onesf", [1, 128], F32)
        scrow = sb("scrow", [1, 1024], F32)
        scb = sb("scb", [128, 1024], F32)
        qsrow = sb("qsrow", [1, 512], F32)
        qsb = sb("qsb", [128, DMODEL], F32)
        uT = sb("uT", [16, T], BF16)
        loraBb = sb("loraBb", [16, DMODEL], BF16)
        loraAb = sb("loraAb", [128, NI * 16], BF16)
        xold0 = sb("xold0", [128, 512], F32)
        xold1 = sb("xold1", [128, 512], F32)
        stage0 = sb("stage0", [128, 512], F32)
        stage1 = sb("stage1", [128, 512], F32)
        ptr0 = ps("ptr0", [128, 1024], BF16)
        ptr1 = ps("ptr1", [128, 1024], BF16)
        pm0 = ps("pm0", [128, 512], F32)
        pm1 = ps("pm1", [128, 512], F32)
        pm2 = ps("pm2", [128, 512], F32)
        pm3 = ps("pm3", [128, 512], F32)
        paux = ps("paux", [128, 512], F32)
        plora = ps("plora", [16, 512], F32)
        pmm = [pm0, pm1, pm2, pm3]
        xtile = [xtile0, xtile1]
        xnbf = [xnbf0, xnbf1]
        xold = [xold0, xold1]
        stage = [stage0, stage1]

        id_tok = s.dma("gp", lambda e: e.dma_start(out=identb[:, :],
                                                   in_=ident[:, :]))
        bias_tok = s.dma("gp", lambda e: e.dma_start(out=biasb[:, :],
                                                     in_=bias[:, :]))
        ones_tok = s.dma("gp", lambda e: e.dma_start(out=onesb[:, :],
                                                     in_=ones_r[:, :]))
        onesf_tok = s.dma("sy", lambda e: e.dma_start(out=onesf[:, :],
                                                      in_=ones_r[:, :]))

        scb_tok = None
        qsb_tok = None
        loraB_tok = None
        loraA_tok = None
        prev_bc_ev = None
        if kind == "ffn0":
            c4 = s.dma("sy", lambda e: e.dma_start(out=scrow[:, :],
                                                   in_=scinv[:, :]))
            for k in range(2):
                mmb = s.add("te", lambda e, k=k: e.matmul(
                    paux[:, :], onesf[0:1, :], scrow[0:1, k * 512:(k + 1) * 512],
                    start=True, stop=True), waits=[c4, onesf_tok, prev_bc_ev],
                    inc="px")
                prev_bc_ev = s.add("vec", lambda e, k=k: e.tensor_copy(
                    scb[:, k * 512:(k + 1) * 512], paux[:, :]), waits=[mmb],
                    inc="pxe")
            scb_tok = prev_bc_ev
        if kind == "ffn1":
            for k in range(8):
                c4 = s.dma("sy", lambda e, k=k: e.dma_start(
                    out=qsrow[:, :], in_=qscale[0:1, k * 512:(k + 1) * 512]),
                    waits=[s.tok("px")])
                mmb = s.add("te", lambda e, k=k: e.matmul(
                    paux[:, :], onesf[0:1, :], qsrow[0:1, :],
                    start=True, stop=True), waits=[c4, onesf_tok, prev_bc_ev],
                    inc="px")
                prev_bc_ev = s.add("vec", lambda e, k=k: e.tensor_copy(
                    qsb[:, k * 512:(k + 1) * 512], paux[:, :]), waits=[mmb],
                    inc="pxe")
            qsb_tok = prev_bc_ev
            c6 = s.dma("gp", lambda e: e.dma_start(out=loraBb[:, :],
                                                   in_=loraBT[:, :]))
            loraB_tok = s.add("vec", lambda e: e.tensor_scalar(
                loraBb[:, :], loraBb[:, :], 2.0, None, ALU.mult), waits=[c6],
                inc="px2")
            loraA_tok = s.dma("gp", lambda e: e.dma_start(
                out=loraAb.ap().rearrange("p (i r) -> p i r", r=16),
                in_=loraAT.ap().rearrange("(i p) r -> p i r", p=128)))

        ln_done = emit_ln_to_xnT(
            s, (xtile, stats, xnbf, xnT, identb, [ptr0, ptr1], id_tok), x)

        lora_done = None
        if kind == "ffn1":
            prev_u_ev = None
            for tch in range(2):
                mm_tok = None
                for i in range(NI):
                    mm_tok = s.add("te", lambda e, i=i, tch=tch: e.matmul(
                        plora[:, :],
                        loraAb[:, i * 16:(i + 1) * 16],
                        xnT[:, i * T + tch * 512: i * T + (tch + 1) * 512],
                        start=(i == 0), stop=(i == NI - 1)),
                        waits=[ln_done, loraA_tok, prev_u_ev]
                        if i == 0 else (), inc="px2" if i == NI - 1 else None)
                prev_u_ev = s.add("vec", lambda e, tch=tch: e.tensor_copy(
                    uT[:, tch * 512:(tch + 1) * 512], plora[:, :]),
                    waits=[mm_tok], inc="pxe2")
            lora_done = prev_u_ev

        evac_tok_bank = [None] * 4
        slab_free = [None] * 4
        store_tok = [None, None]
        evac_read_xold = [None, None]
        for oc in range(8):
            slab_toks = []
            for ig in range(4):
                w = [slab_free[ig]]
                dq_eng = "gp" if kind != "maskmm" else "sy"
                tk = s.dma(dq_eng, lambda e, oc=oc, ig=ig: e.dma_start(
                    out=slab.ap().rearrange("p (g o) -> p g o", o=512)
                    [:, ig * 8:(ig + 1) * 8, :],
                    in_=wT[ig * 1024:(ig + 1) * 1024,
                           oc * 512:(oc + 1) * 512]
                    .rearrange("(g p) o -> p g o", p=128)), waits=w)
                if kind == "ffn0":
                    for g in range(8):
                        itile = ig * 8 + g
                        for ob in range(4):
                            col = itile * 32 + oc * 4 + ob
                            off = (g * 512 + ob * 128) + ig * 4096
                            eng = ("vec", "sc", "gp")[(g * 4 + ob) % 3]
                            if eng == "sc":
                                s.add("sc", lambda e, off=off, col=col:
                                      e.activation(
                                          slab[:, off:off + 128],
                                          slab[:, off:off + 128], AF.Copy,
                                          scale=scb[:, col:col + 1]),
                                      waits=[tk, scb_tok], inc="dq")
                            else:
                                s.add(eng, lambda e, off=off, col=col:
                                      e.tensor_scalar(
                                          slab[:, off:off + 128],
                                          slab[:, off:off + 128],
                                          scb[:, col:col + 1], None, ALU.mult),
                                      waits=[tk, scb_tok], inc="dq")
                    tk = s.tok("dq")
                elif kind == "ffn1":
                    for g in range(8):
                        eng = ("vec", "gp")[g % 2]
                        off = ig * 4096 + g * 512
                        s.add(eng, lambda e, off=off, oc=oc: e.tensor_tensor(
                            slab[:, off:off + 512], slab[:, off:off + 512],
                            qsb[:, oc * 512:(oc + 1) * 512], ALU.mult),
                            waits=[tk, qsb_tok], inc="dq")
                    tk = s.tok("dq")
                slab_toks.append(tk)
            for t in range(NT):
                it = oc * NT + t
                bank = t % 4
                grp_w = [ln_done, evac_tok_bank[bank], bias_tok, ones_tok] \
                    + slab_toks
                if kind == "ffn1":
                    grp_w += [lora_done, loraB_tok]
                s.add("te", lambda e, oc=oc, bank=bank: e.matmul(
                    pmm[bank][:, :], onesb[0:1, :],
                    biasb[0:1, oc * 512:(oc + 1) * 512],
                    start=True, stop=False), waits=grp_w)
                for i in range(NI):
                    is_last = (i == NI - 1) and kind != "ffn1"
                    s.add("te", lambda e, oc=oc, i=i, t=t, bank=bank:
                          e.matmul(
                              pmm[bank][:, :],
                              xnT[:, i * T + t * 128: i * T + (t + 1) * 128],
                              slab[:, i * 512:(i + 1) * 512],
                              start=False, stop=is_last),
                          inc="mm" if is_last else None)
                if kind == "ffn1":
                    s.add("te", lambda e, oc=oc, t=t, bank=bank: e.matmul(
                        pmm[bank][:, :], uT[:, t * 128:(t + 1) * 128],
                        loraBb[:, oc * 512:(oc + 1) * 512],
                        start=False, stop=True), inc="mm")
                mm_tok = s.tok("mm")
                p = it % 2
                xl = s.dma("sy", lambda e, oc=oc, t=t, p=p: e.dma_start(
                    out=xold[p][:, :],
                    in_=x[t * 128:(t + 1) * 128, oc * 512:(oc + 1) * 512]),
                    waits=[evac_read_xold[p]])
                ev = s.add("vec", lambda e, p=p, bank=bank:
                           e.scalar_tensor_tensor(
                               stage[p][:, :], pmm[bank][:, :], half,
                               xold[p][:, :], ALU.mult, ALU.add),
                           waits=[mm_tok, xl, store_tok[p]], inc="ev")
                evac_tok_bank[bank] = ev
                evac_read_xold[p] = ev
                store_tok[p] = s.dma("sy", lambda e, oc=oc, t=t, p=p:
                                     e.dma_start(
                                         out=xo[t * 128:(t + 1) * 128,
                                                oc * 512:(oc + 1) * 512],
                                         in_=stage[p][:, :]), waits=[ev])
            for ig in range(4):
                slab_free[ig] = s.tok("mm")

    s.barrier()


def _prep_common(inputs):
    c = {}
    c["ident"] = np.eye(128, dtype=np.float32)
    c["ones_r"] = np.ones((1, 128), dtype=np.float32)
    sel32 = np.zeros((128, 4), dtype=np.float32)
    for p in range(128):
        sel32[p, p // 32] = 1.0
    c["sel32"] = sel32
    c["rep4"] = np.ascontiguousarray(sel32.T)
    idx = np.arange(128)
    c["tri"] = (idx[None, :] >= idx[:, None]).astype(np.float32)
    selh = np.zeros((32, 4096), dtype=np.float32)
    for h in range(32):
        selh[h, h * 128:(h + 1) * 128] = 1.0
    c["selh"] = selh
    inv_freq = 1.0 / 10000.0 ** (np.arange(0, 128, 2, dtype=np.float64) / 128)
    ang = np.arange(SEQ, dtype=np.float64)[:, None] * inv_freq
    cos = np.concatenate([np.cos(ang), np.cos(ang)], -1)
    sin = np.concatenate([np.sin(ang), np.sin(ang)], -1)
    sinL = sin.copy()
    sinL[:, :64] *= -1.0
    c["cos"] = cos.astype(np.float32)
    c["sinL"] = sinL.astype(np.float32)
    return c


def _run(nc, in_maps):
    res = run_bass_kernel_spmd(nc, in_maps, core_ids=list(range(NCORES)))
    return res.results


def emit_maskprep(s, nc, io):
    masksl = io["masksl"]
    wTsl = io["wTsl"]
    ident = io["ident"]
    sel32 = io["sel32"]
    rep4 = io["rep4"]
    wmT = io["wmT"]
    with contextlib.ExitStack() as stk:
        _p = _stage_pfx()

        def sb(name, shape, dt):
            return stk.enter_context(nc.sbuf_tensor(_p + name, shape, dt))

        def ps(name, shape, dt):
            return stk.enter_context(nc.psum_tensor(_p + name, shape, dt))

        mrow = sb("mrow", [128, DMODEL], F32)
        identf = sb("identf", [128, 128], F32)
        sel32f = sb("sel32f", [128, 4], F32)
        rep4f = sb("rep4f", [4, 128], F32)
        bm1 = sb("bm1", [4, 4 * DMODEL], F32)
        t1 = sb("t1", [128, 16], F32)
        bm2 = sb("bm2", [4, 32 * 16], F32)
        etmp = sb("etmp", [128, 16], F32)
        wt0 = sb("wt0", [128, 512], BF16)
        wt1 = sb("wt1", [128, 512], BF16)
        pa = ps("pa", [128, 512], F32)
        pb = ps("pb", [128, 512], F32)

        c1 = s.dma("sy", lambda e: e.dma_start(out=identf[:, :], in_=ident[:, :]))
        c2 = s.dma("sy", lambda e: e.dma_start(out=sel32f[:, :], in_=sel32[:, :]))
        c3 = s.dma("sy", lambda e: e.dma_start(out=rep4f[:, :], in_=rep4[:, :]))

        prev_ev = None
        prev_mrow_user = None
        for a in range(4):
            ld = s.dma("gp", lambda e, a=a: e.dma_start(
                out=mrow[:, :], in_=masksl[a * 128:(a + 1) * 128, :]),
                waits=[prev_mrow_user])
            for kc in range(8):
                mm = s.add("te", lambda e, kc=kc: e.matmul(
                    pa[0:4, :], sel32f[:, :],
                    mrow[:, kc * 512:(kc + 1) * 512], start=True, stop=True),
                    waits=[ld, c2, prev_ev], inc="t")
                prev_ev = s.add("vec", lambda e, a=a, kc=kc: e.tensor_scalar(
                    bm1[0:4, a * DMODEL + kc * 512: a * DMODEL + (kc + 1) * 512],
                    pa[0:4, :], 1.0, None, ALU.min), waits=[mm], inc="v")
            prev_mrow_user = prev_ev
        bm1_tok = prev_ev

        prev_pa_rd = None
        for kt in range(32):
            trs = None
            for a in range(4):
                trs = s.add("te", lambda e, kt=kt, a=a: e.transpose(
                    pb[:, a * 4:a * 4 + 4],
                    bm1[0:4, a * DMODEL + kt * 128: a * DMODEL + (kt + 1) * 128],
                    identf[0:4, 0:4]),
                    waits=[bm1_tok, c1] + ([s.tok("v")] if a == 0 else []),
                    inc="t")
            ev = s.add("vec", lambda e: e.tensor_copy(t1[:, :], pb[:, 0:16]),
                       waits=[trs], inc="v")
            mm = s.add("te", lambda e: e.matmul(
                pa[0:4, 0:16], sel32f[:, :], t1[:, :], start=True, stop=True),
                waits=[ev, prev_pa_rd], inc="t")
            prev_pa_rd = s.add("vec", lambda e, kt=kt: e.tensor_scalar(
                bm2[0:4, kt * 16:(kt + 1) * 16], pa[0:4, 0:16], 1.0, None,
                ALU.min), waits=[mm], inc="v")
        bm2_tok = prev_pa_rd

        prev_etmp_rd = None
        prev_pb_rd = None
        prev_store = [None, None]
        for kt in range(32):
            mm = s.add("te", lambda e, kt=kt: e.matmul(
                pb[:, 0:16], rep4f[:, :], bm2[0:4, kt * 16:(kt + 1) * 16],
                start=True, stop=True), waits=[bm2_tok, c3, prev_pb_rd],
                inc="t")
            ev = s.add("vec", lambda e: e.tensor_copy(etmp[:, :], pb[:, 0:16]),
                       waits=[mm, prev_etmp_rd], inc="v")
            prev_pb_rd = ev
            p = kt % 2
            wt = [wt0, wt1][p]
            ld = s.dma("gp", lambda e, kt=kt, wt=wt: e.dma_start(
                out=wt[:, :], in_=wTsl[kt * 128:(kt + 1) * 128, :]),
                waits=[prev_store[p]])
            apl = None
            for go in range(16):
                apl = s.add("vec", lambda e, go=go, wt=wt: e.tensor_scalar(
                    wt[:, go * 32:(go + 1) * 32], wt[:, go * 32:(go + 1) * 32],
                    etmp[:, go:go + 1], None, ALU.mult),
                    waits=[ev, ld], inc="v")
            prev_etmp_rd = apl
            prev_store[p] = s.dma("sy", lambda e, kt=kt, wt=wt: e.dma_start(
                out=wmT[kt * 128:(kt + 1) * 128, :], in_=wt[:, :]), waits=[apl])

    s.barrier()


def emit_mamba_a(s, nc, io):
    x = io["x"]
    wkT = io["wkT"]
    wvT = io["wvT"]
    winT = io["winT"]
    dtb = io["dtb"]
    alog = io["alog"]
    cosT = io["cosT"]
    sinLT = io["sinLT"]
    ident = io["ident"]
    ones_r = io["ones_r"]
    xnT_h = io["xnT_h"]
    k_h = io["k_h"]
    v_h = io["v_h"]
    dt_h = io["dt_h"]
    G_h = io["G_h"]
    gt_h = io["gt_h"]
    gtot_h = io["gtot_h"]
    kdf_h = io["kdf_h"]
    C_h = io["C_h"]
    P_h = io["P_h"]
    with contextlib.ExitStack() as stk:
        _p = _stage_pfx()

        def sb(name, shape, dt):
            return stk.enter_context(nc.sbuf_tensor(_p + name, shape, dt))

        def ps(name, shape, dt):
            return stk.enter_context(nc.psum_tensor(_p + name, shape, dt))

        xtile0 = sb("xtile0", [128, DMODEL], F32)
        xtile = [xtile0, xtile0]
        stats = sb("stats", [128, 8], F32)
        xnbf0 = sb("xnbf0", [128, DMODEL], BF16)
        xnbf = [xnbf0, xnbf0]
        xnT = sb("xnT", [128, NI * T], BF16)
        identb = sb("identb", [128, 128], BF16)
        identf = sb("identf", [128, 128], F32)
        onesf = sb("onesf", [1, 128], F32)
        slab = sb("slab", [128, NI * 512], BF16)
        winb = sb("winb", [128, NI * 32], BF16)
        cos_sb = sb("cos_sb", [128, NT * 128], F32)
        sin_sb = sb("sin_sb", [128, NT * 128], F32)
        k_sb = sb("k_sb", [128, NT * 1024], BF16)
        v_sb = sb("v_sb", [128, NT * 1024], BF16)
        tmpA = sb("tmpA", [128, 128], F32)
        tmpB = sb("tmpB", [128, 128], F32)
        z_sb = sb("z_sb", [128, 256], F32)
        za_sb = sb("za_sb", [128, 256], F32)
        dt_sb = sb("dt_sb", [128, 256], F32)
        g_sb = sb("g_sb", [128, 256], F32)
        G_sb = sb("G_sb", [128, 256], F32)
        gts = sb("gts", [32, NCH * 128], F32)
        gtot_row = sb("gtot_row", [1, 256], F32)
        gsum_row = sb("gsum_row", [1, 32], F32)
        p_row = sb("p_row", [1, 32], F32)
        gtotb = sb("gtotb", [128, 32], F32)
        cdb = sb("cdb", [128, 256], F32)
        kdf = sb("kdf", [128, 256], F32)
        dtbb = sb("dtbb", [128, 32], F32)
        ab = sb("ab", [128, 32], F32)
        dtb_row = sb("dtb_row", [1, 32], F32)
        a_row = sb("a_row", [1, 32], F32)
        kdec = sb("kdec", [128, 128], BF16)
        xc = sb("xc", [128, 128], BF16)
        S = sb("S", [128, 128], F32)
        C_sb = sb("C_sb", [128, 32 * 128], F32)
        ptr = [ps("ptr0", [128, 1024], BF16), ps("ptr1", [128, 1024], BF16)]
        pmm = [ps("pm0", [128, 512], F32), ps("pm1", [128, 512], F32),
               ps("pm2", [128, 512], F32), ps("pm3", [128, 512], F32)]
        paux = ps("paux", [128, 512], F32)
        pcs = ps("pcs", [128, 512], F32)

        id_tok = s.dma("gp", lambda e: e.dma_start(out=identb[:, :],
                                                   in_=ident[:, :]))
        idf_tok = s.dma("sy", lambda e: e.dma_start(out=identf[:, :],
                                                    in_=ident[:, :]))
        onesf_tok = s.dma("sy", lambda e: e.dma_start(out=onesf[:, :],
                                                      in_=ones_r[:, :]))
        cs_tok = s.dma("sy", lambda e: e.dma_start(
            out=cos_sb.ap().rearrange("p (t d) -> p t d", d=128),
            in_=cosT.ap().rearrange("(t p) d -> p t d", p=128)))
        sn_tok = s.dma("sy", lambda e: e.dma_start(
            out=sin_sb.ap().rearrange("p (t d) -> p t d", d=128),
            in_=sinLT.ap().rearrange("(t p) d -> p t d", p=128)))
        dtb_tok = s.dma("sy", lambda e: e.dma_start(out=dtb_row[:, :],
                                                    in_=dtb[:, :]))
        a_tok = s.dma("sy", lambda e: e.dma_start(out=a_row[:, :],
                                                  in_=alog[:, :]))
        win_tok = s.dma("gp", lambda e: e.dma_start(
            out=winb.ap().rearrange("p (i h) -> p i h", h=32),
            in_=winT.ap().rearrange("(i p) h -> p i h", p=128)))

        ln_done = emit_ln_to_xnT(
            s, (xtile, stats, xnbf, xnT, identb, ptr, id_tok), x)
        s.dma("sy", lambda e: e.dma_start(
            out=xnT_h.ap().rearrange("(i p) t -> p i t", p=128),
            in_=xnT.ap().rearrange("p (i t) -> p i t", t=T)), waits=[ln_done])

        def proj_1024(wT_t, out_sb, name, slab_free=None):
            evac_tok_bank = [None] * 4
            out_toks = []
            for oc in range(2):
                tk = s.dma("gp", lambda e, oc=oc, wT_t=wT_t: e.dma_start(
                    out=slab.ap().rearrange("p (i o) -> p i o", o=512),
                    in_=wT_t[:, oc * 512:(oc + 1) * 512]
                    .rearrange("(i p) o -> p i o", p=128)), waits=[slab_free])
                for t in range(NT):
                    bank = t % 4
                    s.add("te", lambda e, t=t, bank=bank: e.matmul(
                        pmm[bank][:, :],
                        xnT[:, 0 * T + t * 128: 0 * T + (t + 1) * 128],
                        slab[:, 0:512], start=True, stop=False),
                        waits=[ln_done, tk, evac_tok_bank[bank]])
                    for i in range(1, NI):
                        s.add("te", lambda e, i=i, t=t, bank=bank: e.matmul(
                            pmm[bank][:, :],
                            xnT[:, i * T + t * 128: i * T + (t + 1) * 128],
                            slab[:, i * 512:(i + 1) * 512],
                            start=False, stop=(i == NI - 1)),
                            inc="mm" if i == NI - 1 else None)
                    mm_tok = s.tok("mm")
                    ev = s.add("vec", lambda e, t=t, oc=oc, bank=bank, o=out_sb:
                               e.tensor_copy(
                                   o[:, t * 1024 + oc * 512:
                                     t * 1024 + oc * 512 + 512],
                                   pmm[bank][:, :]), waits=[mm_tok],
                               inc="ev")
                    evac_tok_bank[bank] = ev
                    out_toks.append(ev)
                slab_free = s.tok("mm")
            return s.tok("ev"), slab_free

        k_done, kv_slab_free = proj_1024(wkT, k_sb, "k")
        v_done, v_slab_free = proj_1024(wvT, v_sb, "v",
                                        slab_free=kv_slab_free)

        dt_mm_toks = []
        for t in range(NT):
            s.add("te", lambda e, t=t: e.matmul(
                pcs[:, t * 32:(t + 1) * 32],
                xnT[:, 0 * T + t * 128: 0 * T + (t + 1) * 128],
                winb[:, 0:32], start=True, stop=False),
                waits=[ln_done, win_tok] if t == 0 else [])
            for i in range(1, NI):
                s.add("te", lambda e, i=i, t=t: e.matmul(
                    pcs[:, t * 32:(t + 1) * 32],
                    xnT[:, i * T + t * 128: i * T + (t + 1) * 128],
                    winb[:, i * 32:(i + 1) * 32],
                    start=False, stop=(i == NI - 1)),
                    inc="mm" if i == NI - 1 else None)
            dt_mm_toks.append(s.tok("mm"))

        rope_done = None
        for t in range(NT):
            for kv in range(KVH):
                off = t * 1024 + kv * 128
                co = t * 128
                m1 = s.add("vec", lambda e, off=off, co=co: e.tensor_tensor(
                    tmpA[:, :], k_sb[:, off:off + 128], cos_sb[:, co:co + 128],
                    ALU.mult), waits=[k_done, cs_tok, rope_done], inc="rp")
                m2 = s.add("vec", lambda e, off=off, co=co: e.tensor_tensor(
                    tmpB[:, 0:64], k_sb[:, off + 64:off + 128],
                    sin_sb[:, co:co + 64], ALU.mult), waits=[sn_tok], inc="rp")
                m3 = s.add("vec", lambda e, off=off, co=co: e.tensor_tensor(
                    tmpB[:, 64:128], k_sb[:, off:off + 64],
                    sin_sb[:, co + 64:co + 128], ALU.mult), inc="rp")
                rope_done = s.add("vec", lambda e, off=off: e.tensor_tensor(
                    k_sb[:, off:off + 128], tmpA[:, :], tmpB[:, :], ALU.add),
                    inc="rp")
        k_out_tok = s.dma("sy", lambda e: e.dma_start(
            out=k_h.ap().rearrange("(t p) o -> p t o", p=128),
            in_=k_sb.ap().rearrange("p (t o) -> p t o", o=1024)),
            waits=[rope_done])
        v_out_tok = s.dma("sy", lambda e: e.dma_start(
            out=v_h.ap().rearrange("(t p) o -> p t o", p=128),
            in_=v_sb.ap().rearrange("p (t o) -> p t o", o=1024)),
            waits=[v_done])

        bc1 = s.add("te", lambda e: e.matmul(
            paux[:, 0:32], onesf[0:1, :], dtb_row[0:1, :], start=True,
            stop=True), waits=[dtb_tok, onesf_tok], inc="mm")
        dtbb_tok = s.add("vec", lambda e: e.tensor_copy(dtbb[:, :],
                                                        paux[:, 0:32]),
                         waits=[bc1], inc="ev")
        ae = s.add("sc", lambda e: e.activation(a_row[:, :], a_row[:, :],
                                                AF.Exp), waits=[a_tok],
                   inc="sc")
        an = s.add("vec", lambda e: e.tensor_scalar(
            a_row[:, :], a_row[:, :], -1.0, None, ALU.mult), waits=[ae],
            inc="ev")
        bc2 = s.add("te", lambda e: e.matmul(
            paux[:, 32:64], onesf[0:1, :], a_row[0:1, :], start=True,
            stop=True), waits=[an, onesf_tok, dtbb_tok], inc="mm")
        ab_tok = s.add("vec", lambda e: e.tensor_copy(ab[:, :],
                                                      paux[:, 32:64]),
                       waits=[bc2], inc="ev")

        z_toks = []
        for t in range(NT):
            z_toks.append(s.add("vec", lambda e, t=t: e.tensor_tensor(
                z_sb[:, t * 32:(t + 1) * 32], pcs[:, t * 32:(t + 1) * 32],
                dtbb[:, :], ALU.add), waits=[dt_mm_toks[-1], dtbb_tok],
                inc="ev"))
        sab = s.add("sc", lambda e: e.activation(
            za_sb[:, :], z_sb[:, :], AF.Abs), waits=z_toks, inc="sc")
        sex = s.add("sc", lambda e: e.activation(
            za_sb[:, :], za_sb[:, :], AF.Exp, scale=np.float32(-1.0)),
            waits=[sab], inc="sc")
        sln = s.add("sc", lambda e: e.activation(
            za_sb[:, :], za_sb[:, :], AF.Ln, bias=np.float32(1.0)),
            waits=[sex], inc="sc")
        srl = s.add("vec", lambda e: e.tensor_scalar(
            dt_sb[:, :], z_sb[:, :], 0.0, None, ALU.max), waits=[sln],
            inc="ev")
        sdt = s.add("vec", lambda e: e.tensor_tensor(
            dt_sb[:, :], dt_sb[:, :], za_sb[:, :], ALU.add), waits=[srl],
            inc="ev")
        g_toks = []
        for t in range(NT):
            g_toks.append(s.add("vec", lambda e, t=t: e.tensor_tensor(
                g_sb[:, t * 32:(t + 1) * 32], dt_sb[:, t * 32:(t + 1) * 32],
                ab[:, :], ALU.mult), waits=[sdt, ab_tok], inc="ev"))
        dt_out = s.dma("sy", lambda e: e.dma_start(
            out=dt_h.ap().rearrange("(t p) h -> p t h", p=128),
            in_=dt_sb.ap().rearrange("p (t h) -> p t h", h=32)), waits=[sdt])

        gsum_init = s.add("vec", lambda e: e.memset(gsum_row[:, :], 0.0),
                          inc="ev")
        prev_paux = [ab_tok]
        chunk_toks = []
        for c in range(NCH):
            tr1 = s.add("te", lambda e, c=c: e.transpose(
                paux[0:32, 0:128], g_sb[:, c * 32:(c + 1) * 32],
                identf[:, :]), waits=[g_toks[c], idf_tok] + prev_paux,
                inc="mm")
            scn = s.add("vec", lambda e, c=c: e.tensor_tensor_scan(
                gts[:, c * 128:(c + 1) * 128], paux[0:32, 0:128],
                identf[0:32, 0:128], 0.0, ALU.add, ALU.bypass),
                waits=[tr1], inc="ev")
            tr2 = s.add("te", lambda e, c=c: e.transpose(
                paux[:, 128:160], gts[:, c * 128:(c + 1) * 128]
                if False else gts[0:32, c * 128:(c + 1) * 128],
                identf[0:32, 0:32]), waits=[scn], inc="mm")
            gev = s.add("vec", lambda e, c=c: e.tensor_copy(
                G_sb[:, c * 32:(c + 1) * 32], paux[:, 128:160]),
                waits=[tr2], inc="ev")
            tr3 = s.add("te", lambda e, c=c: e.matmul(
                paux[0:1, 160:192], gts[0:32, c * 128 + 127: c * 128 + 128],
                identf[0:32, 0:32], start=True, stop=True),
                waits=[scn, gev], inc="mm")
            gtv = s.add("vec", lambda e, c=c: e.tensor_copy(
                gtot_row[0:1, c * 32:(c + 1) * 32], paux[0:1, 160:192]),
                waits=[tr3], inc="ev")
            gsm = s.add("vec", lambda e, c=c: e.tensor_tensor(
                gsum_row[:, :], gsum_row[:, :],
                gtot_row[0:1, c * 32:(c + 1) * 32], ALU.add),
                waits=[gtv, gsum_init], inc="ev")
            bcg = s.add("te", lambda e, c=c: e.matmul(
                paux[:, 192:224], onesf[0:1, :],
                gtot_row[0:1, c * 32:(c + 1) * 32], start=True, stop=True),
                waits=[gtv, onesf_tok], inc="mm")
            gbv = s.add("vec", lambda e: e.tensor_copy(gtotb[:, :],
                                                       paux[:, 192:224]),
                        waits=[bcg], inc="ev")
            cdv = s.add("sc", lambda e, c=c: e.activation(
                cdb[:, c * 32:(c + 1) * 32], gtotb[:, :], AF.Exp),
                waits=[gbv], inc="sc")
            kdd = s.add("vec", lambda e, c=c: e.tensor_tensor(
                kdf[:, c * 32:(c + 1) * 32], gtotb[:, :],
                G_sb[:, c * 32:(c + 1) * 32], ALU.subtract),
                waits=[gbv, gev], inc="ev")
            kde = s.add("sc", lambda e, c=c: e.activation(
                kdf[:, c * 32:(c + 1) * 32], kdf[:, c * 32:(c + 1) * 32],
                AF.Exp), waits=[kdd], inc="sc")
            prev_paux = [gev, gtv, gbv, kde, cdv]
            chunk_toks.append((kde, cdv, gev))
        psum_exp = s.add("sc", lambda e: e.activation(
            p_row[:, :], gsum_row[:, :], AF.Exp),
            waits=[s.tok("ev")], inc="sc")
        s.dma("sy", lambda e: e.dma_start(out=P_h[:, :], in_=p_row[:, :]),
              waits=[psum_exp])
        s.dma("sy", lambda e: e.dma_start(out=gtot_h[:, :],
                                          in_=gtot_row[:, :]),
              waits=[s.tok("ev")])
        s.dma("sy", lambda e: e.dma_start(
            out=G_h.ap().rearrange("(t p) h -> p t h", p=128),
            in_=G_sb.ap().rearrange("p (t h) -> p t h", h=32)),
            waits=[s.tok("ev")])
        s.dma("sy", lambda e: e.dma_start(
            out=kdf_h.ap().rearrange("(t p) h -> p t h", p=128),
            in_=kdf.ap().rearrange("p (t h) -> p t h", h=32)),
            waits=[s.tok("sc")])
        s.dma("sy", lambda e: e.dma_start(
            out=gt_h.ap().rearrange("(c g) t -> g c t", g=32),
            in_=gts.ap().rearrange("g (c t) -> g c t", t=128)),
            waits=[s.tok("ev")])

        all_prep = [s.tok("ev"), s.tok("sc"), rope_done]
        prev_cs_rd = None
        for h in range(H):
            kv = h // 4
            sini = s.add("vec", lambda e: e.memset(S[:, :], 0.0),
                         waits=[s.tok("cw") if h > 0 else None], inc="ev")
            for c in range(NCH):
                col = c * 32 + h
                koff = c * 1024 + kv * 128
                kd = s.add("gp", lambda e, koff=koff, col=col: e.tensor_scalar(
                    kdec[:, :], k_sb[:, koff:koff + 128],
                    kdf[:, col:col + 1], None, ALU.mult),
                    waits=all_prep + [prev_cs_rd], inc="gc")
                xv = s.add("gp", lambda e, koff=koff, col=col: e.tensor_scalar(
                    xc[:, :], v_sb[:, koff:koff + 128],
                    dt_sb[:, col:col + 1], None, ALU.mult), inc="gc")
                csm = s.add("te", lambda e: e.matmul(
                    pcs[:, 256:384], kdec[:, :], xc[:, :], start=True,
                    stop=True), waits=[kd, xv, prev_cs_rd], inc="mm")
                prev_cs_rd = s.add("vec", lambda e, col=col: e.scalar_tensor_tensor(
                    S[:, :], S[:, :], cdb[:, col:col + 1], pcs[:, 256:384],
                    ALU.mult, ALU.add), waits=[csm, sini], inc="ev")
            s.add("vec", lambda e, h=h: e.tensor_copy(
                C_sb[:, h * 128:(h + 1) * 128], S[:, :]),
                waits=[prev_cs_rd], inc="cw")
        s.dma("sy", lambda e: e.dma_start(
            out=C_h.ap().rearrange("h d e -> d h e"),
            in_=C_sb.ap().rearrange("d (h e) -> d h e", e=128)),
            waits=[s.tok("cw")])

    s.barrier()


def emit_mamba_b(s, nc, io, dbg=False):
    xnT_h = io["xnT_h"]
    wqT = io["wqT"]
    k_h = io["k_h"]
    v_h = io["v_h"]
    dt_h = io["dt_h"]
    G_h = io["G_h"]
    gt_h = io["gt_h"]
    gtot_h = io["gtot_h"]
    kdf_h = io["kdf_h"]
    C_all = io["C_all"]
    P_row = io["P_row"]
    m_row = io["m_row"]
    invm_row = io["invm_row"]
    m_bcd = io["m_bcd"]
    selh = io["selh"]
    tri = io["tri"]
    gnorm = io["gnorm"]
    cosT = io["cosT"]
    sinLT = io["sinLT"]
    ident = io["ident"]
    ones_r = io["ones_r"]
    yr_h = io["yr_h"]
    if dbg:
        dbg_h = io["dbg_h"]
    with contextlib.ExitStack() as stk:
        _p = _stage_pfx()

        def sb(name, shape, dt):
            return stk.enter_context(nc.sbuf_tensor(_p + name, shape, dt))

        def ps(name, shape, dt):
            return stk.enter_context(nc.psum_tensor(_p + name, shape, dt))

        q_sb = sb("q_sb", [128, NT * DMODEL], BF16)
        cos_sb = sb("cos_sb", [128, NT * 128], F32)
        sin_sb = sb("sin_sb", [128, NT * 128], F32)
        identb = sb("identb", [128, 128], BF16)
        identf = sb("identf", [128, 128], F32)
        onesf = sb("onesf", [1, 128], F32)
        tmpA = sb("tmpA", [128, 128], F32)
        tmpB = sb("tmpB", [128, 128], F32)
        rows = sb("rows", [1, 1024], F32)
        gtrow = sb("gtrow", [1, 256], F32)
        gnb = sb("gnb", [128, 128], F32)
        tri_sb = sb("tri_sb", [128, 128], F32)
        m_bc = sb("m_bc", [128, 8], F32)
        qtT = sb("qtT", [128, 128], BF16)
        kTt = sb("kTt", [128, 128], BF16)
        dec0 = sb("dec0", [128, 128], F32)
        dec_sb = sb("dec_sb", [128, 128], F32)
        dm = sb("dm", [128, 128], F32)
        st_sb = sb("st_sb", [128, 128], BF16)
        xc_sb = sb("xc_sb", [128, 128], BF16)
        kdec = sb("kdec", [128, 128], BF16)
        sc_sb = sb("sc_sb", [128, 128], BF16)
        yfull = sb("yfull", [128, 128], F32)
        zc_sb = sb("zc_sb", [128, 128], F32)
        y1_sb = sb("y1_sb", [128, 128], F32)
        tmpc = sb("tmpc", [128, 128], F32)
        selh_sb = sb("selh_sb", [32, 128], F32)
        ss = sb("ss", [128, 1], F32)
        rr = sb("rr", [128, 1], F32)
        dbg_sb = sb("dbg_sb", [128, 24 * 128], F32) if dbg else None
        pmm = [ps("pm0", [128, 512], F32), ps("pm1", [128, 512], F32),
               ps("pm2", [128, 512], F32), ps("pm3", [128, 512], F32)]
        pz = ps("pz", [128, 512], F32)
        ptr = [ps("ptr0", [128, 1024], BF16), ps("ptr1", [128, 1024], BF16)]
        paux = ps("paux", [128, 512], F32)
        pgb, psc, py, pcs = pmm[0], pmm[1], pmm[2], pmm[3]

        id_tok = s.dma("gp", lambda e: e.dma_start(out=identb[:, :],
                                                   in_=ident[:, :]))
        idf_tok = s.dma("sy", lambda e: e.dma_start(out=identf[:, :],
                                                    in_=ident[:, :]))
        onesf_tok = s.dma("sy", lambda e: e.dma_start(out=onesf[:, :],
                                                      in_=ones_r[:, :]))
        cs_tok = s.dma("sy", lambda e: e.dma_start(
            out=cos_sb.ap().rearrange("p (t d) -> p t d", d=128),
            in_=cosT.ap().rearrange("(t p) d -> p t d", p=128)))
        sn_tok = s.dma("sy", lambda e: e.dma_start(
            out=sin_sb.ap().rearrange("p (t d) -> p t d", d=128),
            in_=sinLT.ap().rearrange("(t p) d -> p t d", p=128)))
        tri_tok = s.dma("sy", lambda e: e.dma_start(out=tri_sb[:, :],
                                                    in_=tri[:, :]))
        mbc_tok = s.dma("sy", lambda e: e.dma_start(out=m_bc[:, :],
                                                    in_=m_bcd[:, :]))
        r1 = s.dma("sy", lambda e: e.dma_start(out=rows[0:1, 0:256],
                                               in_=P_row[:, :]))
        r2 = s.dma("sy", lambda e: e.dma_start(out=rows[0:1, 256:512],
                                               in_=m_row[:, :]))
        r3 = s.dma("sy", lambda e: e.dma_start(out=rows[0:1, 512:768],
                                               in_=invm_row[:, :]))
        r4 = s.dma("sy", lambda e: e.dma_start(out=gtrow[:, :],
                                               in_=gtot_h[:, :]))
        gn_tok = s.dma("sy", lambda e: e.dma_start(out=gnb[0:1, :],
                                                   in_=gnorm[:, :]))

        cm_slab = nc.sbuf_tensor(_p + "slab", [128, NI * 512], BF16)
        slab = cm_slab.__enter__()
        cm_xnT = nc.sbuf_tensor(_p + "xnT", [128, NI * T], BF16)
        xnT = cm_xnT.__enter__()
        xn_tok = s.dma("sy", lambda e: e.dma_start(
            out=xnT.ap().rearrange("p (i t) -> p i t", t=T),
            in_=xnT_h.ap().rearrange("(i p) t -> p i t", p=128)))
        evac_tok_bank = [None] * 4
        slab_free = None
        for oc in range(8):
            tk = s.dma("gp", lambda e, oc=oc: e.dma_start(
                out=slab.ap().rearrange("p (i o) -> p i o", o=512),
                in_=wqT[:, oc * 512:(oc + 1) * 512]
                .rearrange("(i p) o -> p i o", p=128)), waits=[slab_free])
            for t in range(NT):
                bank = t % 4
                s.add("te", lambda e, t=t, bank=bank: e.matmul(
                    pmm[bank][:, :],
                    xnT[:, 0 * T + t * 128: 0 * T + (t + 1) * 128],
                    slab[:, 0:512], start=True, stop=False),
                    waits=[xn_tok, tk, evac_tok_bank[bank]])
                for i in range(1, NI):
                    s.add("te", lambda e, i=i, t=t, bank=bank: e.matmul(
                        pmm[bank][:, :],
                        xnT[:, i * T + t * 128: i * T + (t + 1) * 128],
                        slab[:, i * 512:(i + 1) * 512],
                        start=False, stop=(i == NI - 1)),
                        inc="mm" if i == NI - 1 else None)
                mm_tok = s.tok("mm")
                ev = s.add("vec", lambda e, t=t, oc=oc, bank=bank:
                           e.tensor_copy(
                               q_sb[:, t * DMODEL + oc * 512:
                                    t * DMODEL + oc * 512 + 512],
                               pmm[bank][:, :]), waits=[mm_tok], inc="ev")
                evac_tok_bank[bank] = ev
            slab_free = s.tok("mm")
        q_done = s.tok("ev")
        qmm_done = s.tok("mm")
        cm_xnT.__exit__(None, None, None)
        cm_slab.__exit__(None, None, None)

        k_sb = sb("k_sb", [128, NT * 1024], BF16)
        v_sb = sb("v_sb", [128, NT * 1024], BF16)
        S_all = sb("S_all", [128, 32 * 128], F32)
        cslab = sb("cslab", [128, 32 * 128], F32)
        gts = sb("gts", [32, NCH * 128], F32)
        dt_sb = sb("dt_sb", [128, 256], F32)
        G_sb = sb("G_sb", [128, 256], F32)
        negG = sb("negG", [128, 256], F32)
        expG = sb("expG", [128, 256], F32)
        kdf = sb("kdf", [128, 256], F32)
        cdb = sb("cdb", [128, 256], F32)
        ptilb = sb("ptilb", [128, 256], F32)

        kld = s.dma("sy", lambda e: e.dma_start(
            out=k_sb.ap().rearrange("p (t o) -> p t o", o=1024),
            in_=k_h.ap().rearrange("(t p) o -> p t o", p=128)),
            waits=[qmm_done])
        vld = s.dma("sy", lambda e: e.dma_start(
            out=v_sb.ap().rearrange("p (t o) -> p t o", o=1024),
            in_=v_h.ap().rearrange("(t p) o -> p t o", p=128)),
            waits=[qmm_done])
        dld = s.dma("sy", lambda e: e.dma_start(
            out=dt_sb.ap().rearrange("p (t h) -> p t h", h=32),
            in_=dt_h.ap().rearrange("(t p) h -> p t h", p=128)),
            waits=[qmm_done])
        gld = s.dma("sy", lambda e: e.dma_start(
            out=G_sb.ap().rearrange("p (t h) -> p t h", h=32),
            in_=G_h.ap().rearrange("(t p) h -> p t h", p=128)),
            waits=[qmm_done])
        gtld = s.dma("sy", lambda e: e.dma_start(
            out=gts.ap().rearrange("g (c t) -> g c t", t=128),
            in_=gt_h.ap().rearrange("(c g) t -> g c t", g=32)),
            waits=[qmm_done])
        kdld = s.dma("sy", lambda e: e.dma_start(
            out=kdf.ap().rearrange("p (t h) -> p t h", h=32),
            in_=kdf_h.ap().rearrange("(t p) h -> p t h", p=128)),
            waits=[qmm_done])

        rope_done = None
        for t in range(NT):
            for h in range(H):
                off = t * DMODEL + h * 128
                co = t * 128
                s.add("vec", lambda e, off=off, co=co: e.tensor_tensor(
                    tmpA[:, :], q_sb[:, off:off + 128], cos_sb[:, co:co + 128],
                    ALU.mult), waits=[q_done, cs_tok, sn_tok, rope_done],
                    inc="rp")
                s.add("vec", lambda e, off=off, co=co: e.tensor_tensor(
                    tmpB[:, 0:64], q_sb[:, off + 64:off + 128],
                    sin_sb[:, co:co + 64], ALU.mult), inc="rp")
                s.add("vec", lambda e, off=off, co=co: e.tensor_tensor(
                    tmpB[:, 64:128], q_sb[:, off:off + 64],
                    sin_sb[:, co + 64:co + 128], ALU.mult), inc="rp")
                rope_done = s.add("vec", lambda e, off=off: e.tensor_tensor(
                    q_sb[:, off:off + 128], tmpA[:, :], tmpB[:, :], ALU.add),
                    inc="rp")

        ng_tok = s.add("vec", lambda e: e.tensor_scalar(
            negG[:, :], G_sb[:, :], -1.0, None, ALU.mult), waits=[gld],
            inc="ev")
        eg_tok = s.add("sc", lambda e: e.activation(
            expG[:, :], G_sb[:, :], AF.Exp), waits=[gld], inc="sc")
        bc1 = s.add("te", lambda e: e.matmul(
            paux[:, 0:256], onesf[0:1, :], gtrow[0:1, :], start=True,
            stop=True), waits=[r4, onesf_tok], inc="tt")
        cdv = s.add("vec", lambda e: e.tensor_copy(cdb[:, :], paux[:, 0:256]),
                    waits=[bc1], inc="ev")
        cde = s.add("sc", lambda e: e.activation(
            cdb[:, :], cdb[:, :], AF.Exp), waits=[cdv], inc="sc")
        bc2 = s.add("te", lambda e: e.matmul(
            paux[:, 256:384], onesf[0:1, :], gnb[0:1, :], start=True,
            stop=True), waits=[gn_tok, onesf_tok, cdv], inc="tt")
        gnb_tok = s.add("vec", lambda e: e.tensor_copy(gnb[:, :],
                                                       paux[:, 256:384]),
                        waits=[bc2], inc="ev")
        pt1 = s.add("vec", lambda e: e.tensor_tensor(
            rows[0:1, 768:1024], rows[0:1, 0:256], rows[0:1, 256:512],
            ALU.mult), waits=[r1, r2], inc="ev")
        pt2 = s.add("vec", lambda e: e.tensor_tensor(
            rows[0:1, 768:1024], rows[0:1, 768:1024], rows[0:1, 512:768],
            ALU.add), waits=[pt1, r3], inc="ev")
        bc3 = s.add("te", lambda e: e.matmul(
            paux[:, 0:256], onesf[0:1, :], rows[0:1, 768:1024], start=True,
            stop=True), waits=[pt2, cdv, gnb_tok], inc="tt")
        ptb_tok = s.add("vec", lambda e: e.tensor_copy(ptilb[:, :],
                                                       paux[:, 0:256]),
                        waits=[bc3], inc="ev")

        sini = s.add("vec", lambda e: e.memset(S_all[:, :], 0.0), inc="ev")
        prev_supd = sini
        for j in range(8):
            cj = s.dma("sy", lambda e, j=j: e.dma_start(
                out=cslab.ap().rearrange("d (h e) -> d h e", e=128),
                in_=C_all[j].rearrange("h d e -> d h e")),
                waits=[prev_supd])
            for h in range(H):
                tm = s.add("vec", lambda e, h=h, j=j: e.tensor_scalar(
                    tmpc[:, :], cslab[:, h * 128:(h + 1) * 128],
                    m_bc[:, j:j + 1], None, ALU.mult),
                    waits=[cj, mbc_tok], inc="ev")
                prev_supd = s.add("vec", lambda e, h=h, j=j:
                                  e.scalar_tensor_tensor(
                                      S_all[:, h * 128:(h + 1) * 128],
                                      S_all[:, h * 128:(h + 1) * 128],
                                      ptilb[:, j * 32 + h:j * 32 + h + 1],
                                      tmpc[:, :], ALU.mult, ALU.add),
                                  waits=[tm, ptb_tok, sini], inc="ev")
        sin_done = prev_supd

        prep_done = [sin_done, s.tok("ev"), s.tok("sc"), rope_done, dld,
                     kdld, vld, kld]
        tok_qtT_rd = None
        tok_kTt_rd = None
        tok_ptr_ev = [None, None]
        tok_pgb_rd = None
        tok_dec_rd = None
        tok_dm_rd = None
        tok_psc_rd = None
        tok_st_rd = None
        tok_xc_rd = None
        tok_py_rd = None
        tok_pz_rd = None
        tok_sc_rd = None
        tok_pcs_rd = None
        tok_kdec_rd = None
        tok_yfull_rd = None
        tok_y1_rd = None
        tok_selh_rd = None
        prev_selh_h = -1
        selh_tok = None
        for h in range(H):
            kv = h // 4
            selh_tok = s.dma("sy", lambda e, h=h: e.dma_start(
                out=selh_sb[:, :], in_=selh[:, h * 128:(h + 1) * 128]),
                waits=[tok_selh_rd])
            for c in range(NCH):
                col = c * 32 + h
                koff = c * 1024 + kv * 128
                qoff = c * DMODEL + h * 128
                t1 = s.add("te", lambda e, qoff=qoff: e.transpose(
                    ptr[0][:, 0:128], q_sb[:, qoff:qoff + 128],
                    identb[:, :]),
                    waits=prep_done + [tok_ptr_ev[0], id_tok], inc="tt")
                v2 = s.add("vec", lambda e: e.tensor_copy(
                    qtT[:, :], ptr[0][:, 0:128]),
                    waits=[t1, tok_qtT_rd], inc="ev")
                tok_ptr_ev[0] = v2
                t1b = s.add("te", lambda e, koff=koff: e.transpose(
                    ptr[1][:, 0:128], k_sb[:, koff:koff + 128],
                    identb[:, :]),
                    waits=[kld, id_tok, tok_ptr_ev[1]], inc="tt")
                v2b = s.add("vec", lambda e: e.tensor_copy(
                    kTt[:, :], ptr[1][:, 0:128]),
                    waits=[t1b, tok_kTt_rd], inc="ev")
                tok_ptr_ev[1] = v2b
                t2 = s.add("te", lambda e, c=c: e.matmul(
                    pgb[:, 0:128], selh_sb[:, :],
                    gts[0:32, c * 128:(c + 1) * 128], start=True, stop=True),
                    waits=[selh_tok, gtld, tok_pgb_rd], inc="tt")
                tok_selh_rd = t2
                v0 = s.add("vec", lambda e, col=col: e.tensor_scalar(
                    dec0[:, :], pgb[:, 0:128], negG[:, col:col + 1],
                    0.0, ALU.add, ALU.min),
                    waits=[t2, ng_tok, tok_dec_rd], inc="ev")
                tok_pgb_rd = v0
                s1 = s.add("sc", lambda e: e.activation(
                    dec_sb[:, :], dec0[:, :], AF.Exp), waits=[v0, tok_dm_rd],
                    inc="sc")
                v3 = s.add("vec", lambda e: e.tensor_tensor(
                    dm[:, :], dec_sb[:, :], tri_sb[:, :], ALU.mult),
                    waits=[s1, tri_tok, tok_dm_rd], inc="ev")
                tok_dec_rd = v3
                t3 = s.add("te", lambda e: e.matmul(
                    psc[:, 0:128], kTt[:, :], qtT[:, :],
                    start=True, stop=True),
                    waits=[v2, v2b, tok_psc_rd], inc="tt")
                tok_kTt_rd = t3
                v4 = s.add("vec", lambda e: e.tensor_tensor(
                    st_sb[:, :], dm[:, :], psc[:, 0:128], ALU.mult),
                    waits=[v3, t3, tok_st_rd], inc="ev")
                tok_dm_rd = v4
                tok_psc_rd = v4
                v5 = s.add("vec", lambda e, koff=koff, col=col: e.tensor_scalar(
                    xc_sb[:, :], v_sb[:, koff:koff + 128],
                    dt_sb[:, col:col + 1], None, ALU.mult),
                    waits=[tok_xc_rd], inc="ev")
                t4 = s.add("te", lambda e: e.matmul(
                    py[:, 0:128], st_sb[:, :], xc_sb[:, :], start=True,
                    stop=True), waits=[v4, v5, tok_py_rd], inc="tt")
                tok_st_rd = t4
                s2 = s.add("sc", lambda e, h=h: e.activation(
                    sc_sb[:, :], S_all[:, h * 128:(h + 1) * 128], AF.Copy,
                    bias=0.0), waits=[tok_sc_rd, s.tok("ev")], inc="sc")
                t5 = s.add("te", lambda e: e.matmul(
                    pz[:, 0:128], qtT[:, :], sc_sb[:, :], start=True,
                    stop=True), waits=[s2, v2, tok_pz_rd], inc="tt")
                tok_qtT_rd = t5
                tok_sc_rd = t5
                g1 = s.add("gp", lambda e, koff=koff, col=col: e.tensor_scalar(
                    kdec[:, :], k_sb[:, koff:koff + 128],
                    kdf[:, col:col + 1], None, ALU.mult),
                    waits=[kld, kdld, tok_kdec_rd], inc="gc")
                t6 = s.add("te", lambda e: e.matmul(
                    pcs[:, 0:128], kdec[:, :], xc_sb[:, :], start=True,
                    stop=True), waits=[g1, v5, tok_pcs_rd], inc="tt")
                tok_kdec_rd = t6
                tok_xc_rd = t6
                v7 = s.add("vec", lambda e, h=h, col=col:
                           e.scalar_tensor_tensor(
                               S_all[:, h * 128:(h + 1) * 128],
                               S_all[:, h * 128:(h + 1) * 128],
                               cdb[:, col:col + 1], pcs[:, 0:128],
                               ALU.mult, ALU.add),
                           waits=[t6, s2, cde], inc="ev")
                tok_pcs_rd = v7
                v6a = s.add("vec", lambda e: e.tensor_copy(
                    zc_sb[:, :], pz[:, 0:128]),
                    waits=[t5, tok_pz_rd], inc="ev")
                v6 = s.add("vec", lambda e, col=col: e.scalar_tensor_tensor(
                    yfull[:, :], zc_sb[:, :], expG[:, col:col + 1],
                    py[:, 0:128], ALU.mult, ALU.add),
                    waits=[t4, v6a, eg_tok, tok_yfull_rd], inc="ev")
                tok_py_rd = v6
                tok_pz_rd = v6a
                s3 = s.add("sc", lambda e: e.activation(
                    y1_sb[:, :], yfull[:, :], AF.Square, accum_out=ss[:, :]),
                    waits=[v6, tok_y1_rd], inc="sc")
                s4 = s.add("sc", lambda e: e.activation(
                    rr[:, :], ss[:, :], AF.Ln, scale=np.float32(1.0 / 128.0),
                    bias=np.float32(RMS_EPS)), waits=[s3], inc="sc")
                s5 = s.add("sc", lambda e: e.activation(
                    rr[:, :], rr[:, :], AF.Exp, scale=np.float32(-0.5)),
                    waits=[s4], inc="sc")
                v8 = s.add("vec", lambda e: e.tensor_scalar(
                    y1_sb[:, :], yfull[:, :], rr[:, :], None, ALU.mult),
                    waits=[s5, s3], inc="ev")
                tok_yfull_rd = v8
                v9 = s.add("vec", lambda e, qoff=qoff: e.tensor_tensor(
                    q_sb[:, qoff:qoff + 128], y1_sb[:, :], gnb[:, :],
                    ALU.mult), waits=[v8, gnb_tok, t5], inc="ev")
                tok_y1_rd = v9
                if dbg and (h, c) in ((0, 0), (0, 1)):
                    base = 0 if c == 0 else 12 * 128
                    def dcp(slot, buf, wt):
                        tok = s.add("vec",
                                    lambda e, slot=slot, buf=buf, base=base:
                                    e.tensor_copy(
                                        dbg_sb[:, base + slot * 128:
                                               base + (slot + 1) * 128],
                                        buf), waits=wt, inc="ev")
                        return tok
                    dcp(0, qtT[:, :], [v2])
                    dcp(1, kTt[:, :], [v2b])
                    dcp(2, dec_sb[:, :], [s1])
                    dcp(3, dm[:, :], [v3])
                    w = dcp(4, psc[:, 0:128], [t3])
                    tok_psc_rd = w
                    dcp(5, st_sb[:, :], [v4])
                    dcp(6, xc_sb[:, :], [v5])
                    dcp(7, yfull[:, :], [v6])
                    w = dcp(8, py[:, 0:128], [t4])
                    tok_py_rd = w
                    dcp(9, zc_sb[:, :], [v6a])
                    dcp(10, S_all[:, 0:128], [v7])
                    w = dcp(11, pcs[:, 0:128], [t6])
                    tok_pcs_rd = w
        s.dma("sy", lambda e: e.dma_start(
            out=yr_h.ap().rearrange("(t p) o -> p t o", p=128),
            in_=q_sb.ap().rearrange("p (t o) -> p t o", o=DMODEL)),
            waits=[s.tok("ev")])
        if dbg:
            s.dma("sy", lambda e: e.dma_start(out=dbg_h[:, :],
                                              in_=dbg_sb[:, :]),
                  waits=[s.tok("ev")])

    s.barrier()


def emit_mamba_c(s, nc, io):
    x = io["x"]
    xnT_h = io["xnT_h"]
    yr_h = io["yr_h"]
    wgT = io["wgT"]
    woT = io["woT"]
    ident = io["ident"]
    xo = io["xo"]
    with contextlib.ExitStack() as stk:
        _p = _stage_pfx()

        def sb(name, shape, dt):
            return stk.enter_context(nc.sbuf_tensor(_p + name, shape, dt))

        def ps(name, shape, dt):
            return stk.enter_context(nc.psum_tensor(_p + name, shape, dt))

        yh_sb = sb("yh_sb", [128, NT * DMODEL], BF16)
        identb = sb("identb", [128, 128], BF16)
        yrt0 = sb("yrt0", [128, 512], BF16)
        yrt1 = sb("yrt1", [128, 512], BF16)
        sg0 = sb("sg0", [128, 512], BF16)
        sg1 = sb("sg1", [128, 512], BF16)
        xold0 = sb("xold0", [128, 512], F32)
        xold1 = sb("xold1", [128, 512], F32)
        stage0 = sb("stage0", [128, 512], F32)
        stage1 = sb("stage1", [128, 512], F32)
        pmm = [ps("pm0", [128, 512], F32), ps("pm1", [128, 512], F32),
               ps("pm2", [128, 512], F32), ps("pm3", [128, 512], F32)]
        ptr = [ps("ptr0", [128, 1024], BF16), ps("ptr1", [128, 1024], BF16)]
        id_tok = s.dma("gp", lambda e: e.dma_start(out=identb[:, :],
                                                   in_=ident[:, :]))

        cm_slab = nc.sbuf_tensor(_p + "slab", [128, NI * 512], BF16)
        slab = cm_slab.__enter__()
        cm_xnT = nc.sbuf_tensor(_p + "xnT", [128, NI * T], BF16)
        xnT = cm_xnT.__enter__()
        xn_tok = s.dma("sy", lambda e: e.dma_start(
            out=xnT.ap().rearrange("p (i t) -> p i t", t=T),
            in_=xnT_h.ap().rearrange("(i p) t -> p i t", p=128)))
        evac_tok_bank = [None] * 4
        slab_free = None
        yr_rd = [None, None]
        sg_rd = [None, None]
        for oc in range(8):
            tk = s.dma("gp", lambda e, oc=oc: e.dma_start(
                out=slab.ap().rearrange("p (i o) -> p i o", o=512),
                in_=wgT[:, oc * 512:(oc + 1) * 512]
                .rearrange("(i p) o -> p i o", p=128)), waits=[slab_free])
            for t in range(NT):
                it = oc * NT + t
                bank = t % 4
                p = it % 2
                s.add("te", lambda e, t=t, bank=bank: e.matmul(
                    pmm[bank][:, :],
                    xnT[:, 0 * T + t * 128: 0 * T + (t + 1) * 128],
                    slab[:, 0:512], start=True, stop=False),
                    waits=[xn_tok, tk, evac_tok_bank[bank]])
                for i in range(1, NI):
                    s.add("te", lambda e, i=i, t=t, bank=bank: e.matmul(
                        pmm[bank][:, :],
                        xnT[:, i * T + t * 128: i * T + (t + 1) * 128],
                        slab[:, i * 512:(i + 1) * 512],
                        start=False, stop=(i == NI - 1)),
                        inc="mm" if i == NI - 1 else None)
                mm_tok = s.tok("mm")
                ld = s.dma("sy", lambda e, t=t, oc=oc, p=p: e.dma_start(
                    out=[yrt0, yrt1][p][:, :],
                    in_=yr_h[t * 128:(t + 1) * 128,
                             oc * 512:(oc + 1) * 512]), waits=[yr_rd[p]])
                sg = s.add("sc", lambda e, bank=bank, p=p: e.activation(
                    [sg0, sg1][p][:, :], pmm[bank][:, :], AF.Silu),
                    waits=[mm_tok, sg_rd[p]], inc="sc")
                ev = s.add("vec", lambda e, t=t, oc=oc, p=p: e.tensor_tensor(
                    yh_sb[:, t * DMODEL + oc * 512: t * DMODEL + oc * 512 + 512],
                    [sg0, sg1][p][:, :], [yrt0, yrt1][p][:, :], ALU.mult),
                    waits=[sg, ld], inc="ev")
                yr_rd[p] = ev
                sg_rd[p] = ev
                evac_tok_bank[bank] = s.add(
                    "vec", lambda e: e.engine_nop(), waits=[ev], inc="ev")                     if False else ev
            slab_free = s.tok("mm")
        gate_done = s.tok("ev")
        gmm_done = s.tok("mm")
        cm_xnT.__exit__(None, None, None)
        cm_slab.__exit__(None, None, None)

        yhT = sb("yhT", [128, NI * T], BF16)
        slab2 = sb("slab2", [128, NI * 512], BF16)
        prev_ptr = [None, None]
        evs = []
        for n in range(NT * NI):
            t, i = divmod(n, NI)
            pb = n % 2
            tr = s.add("te", lambda e, t=t, i=i, pb=pb: e.transpose(
                ptr[pb][:, 0:128],
                yh_sb[:, t * DMODEL + i * 128: t * DMODEL + (i + 1) * 128],
                identb[:, :]),
                waits=[gate_done, id_tok, prev_ptr[pb]], inc="tt")
            ev = s.add("vec", lambda e, t=t, i=i, pb=pb: e.tensor_copy(
                yhT[:, i * T + t * 128: i * T + (t + 1) * 128],
                ptr[pb][:, 0:128]), waits=[tr], inc="ev")
            prev_ptr[pb] = ev
            evs.append(ev)
        yhT_done = s.tok("ev")

        evac_tok_bank = [None] * 4
        slab_free = None
        store_tok = [None, None]
        evac_read_xold = [None, None]
        for oc in range(8):
            tk = s.dma("gp", lambda e, oc=oc: e.dma_start(
                out=slab2.ap().rearrange("p (i o) -> p i o", o=512),
                in_=woT[:, oc * 512:(oc + 1) * 512]
                .rearrange("(i p) o -> p i o", p=128)),
                waits=[slab_free if slab_free is not None else gmm_done])
            for t in range(NT):
                it = oc * NT + t
                bank = t % 4
                p = it % 2
                s.add("te", lambda e, t=t, bank=bank: e.matmul(
                    pmm[bank][:, :],
                    yhT[:, 0 * T + t * 128: 0 * T + (t + 1) * 128],
                    slab2[:, 0:512], start=True, stop=False),
                    waits=[yhT_done, tk, evac_tok_bank[bank]])
                for i in range(1, NI):
                    s.add("te", lambda e, i=i, t=t, bank=bank: e.matmul(
                        pmm[bank][:, :],
                        yhT[:, i * T + t * 128: i * T + (t + 1) * 128],
                        slab2[:, i * 512:(i + 1) * 512],
                        start=False, stop=(i == NI - 1)),
                        inc="mm" if i == NI - 1 else None)
                mm_tok = s.tok("mm")
                xl = s.dma("sy", lambda e, oc=oc, t=t, p=p: e.dma_start(
                    out=[xold0, xold1][p][:, :],
                    in_=x[t * 128:(t + 1) * 128, oc * 512:(oc + 1) * 512]),
                    waits=[evac_read_xold[p]])
                ev = s.add("vec", lambda e, p=p, bank=bank:
                           e.scalar_tensor_tensor(
                               [stage0, stage1][p][:, :], pmm[bank][:, :],
                               1.0, [xold0, xold1][p][:, :], ALU.mult,
                               ALU.add),
                           waits=[mm_tok, xl, store_tok[p]], inc="ev")
                evac_tok_bank[bank] = ev
                evac_read_xold[p] = ev
                store_tok[p] = s.dma("sy", lambda e, oc=oc, t=t, p=p:
                                     e.dma_start(
                                         out=xo[t * 128:(t + 1) * 128,
                                                oc * 512:(oc + 1) * 512],
                                         in_=[stage0, stage1][p][:, :]),
                                     waits=[ev])
            slab_free = s.tok("mm")

    s.barrier()


def build_launch_A():
    nc = bass.Bass("TRN2", target_bir_lowering=False)
    E_IN, E_OUT = "ExternalInput", "ExternalOutput"
    io = {}
    io["ident"] = nc.dram_tensor("ident", [128, 128], F32, kind=E_IN)
    io["ones_r"] = nc.dram_tensor("ones_r", [1, 128], F32, kind=E_IN)
    x0 = nc.dram_tensor("x0", [T, DMODEL], F32, kind=E_IN)
    w0T = nc.dram_tensor("w0T", [DMODEL, DMODEL], F32, kind=E_IN)
    scinv = nc.dram_tensor("scinv", [1, 1024], F32, kind=E_IN)
    b0 = nc.dram_tensor("b0", [1, DMODEL], F32, kind=E_IN)
    x1 = nc.dram_tensor("x1", [T, DMODEL], F32, kind=E_OUT)
    ma = {
        "wkT": nc.dram_tensor("wkT", [DMODEL, 1024], F32, kind=E_IN),
        "wvT": nc.dram_tensor("wvT", [DMODEL, 1024], F32, kind=E_IN),
        "winT": nc.dram_tensor("winT", [DMODEL, 32], F32, kind=E_IN),
        "dtb": nc.dram_tensor("dtb", [1, 32], F32, kind=E_IN),
        "alog": nc.dram_tensor("alog", [1, 32], F32, kind=E_IN),
        "cosT": nc.dram_tensor("cosT", [T, 128], F32, kind=E_IN),
        "sinLT": nc.dram_tensor("sinLT", [T, 128], F32, kind=E_IN),
        "xnT_h": nc.dram_tensor("xnT_h", [DMODEL, T], BF16, kind=E_OUT),
        "k_h": nc.dram_tensor("k_h", [T, 1024], BF16, kind=E_OUT),
        "v_h": nc.dram_tensor("v_h", [T, 1024], BF16, kind=E_OUT),
        "dt_h": nc.dram_tensor("dt_h", [T, 32], F32, kind=E_OUT),
        "G_h": nc.dram_tensor("G_h", [T, 32], F32, kind=E_OUT),
        "gt_h": nc.dram_tensor("gt_h", [NCH * 32, 128], F32, kind=E_OUT),
        "gtot_h": nc.dram_tensor("gtot_h", [1, 256], F32, kind=E_OUT),
        "kdf_h": nc.dram_tensor("kdf_h", [T, 32], F32, kind=E_OUT),
        "C_h": nc.dram_tensor("C_h", [32, 128, 128], F32, kind=E_OUT),
        "P_h": nc.dram_tensor("P_h", [1, 32], F32, kind=E_OUT),
    }
    mp = {
        "masksl": nc.dram_tensor("masksl", [512, DMODEL], I32, kind=E_IN),
        "wTsl": nc.dram_tensor("wTsl", [DMODEL, 512], F32, kind=E_IN),
        "sel32": nc.dram_tensor("sel32", [128, 4], F32, kind=E_IN),
        "rep4": nc.dram_tensor("rep4", [4, 128], F32, kind=E_IN),
        "wmT": nc.dram_tensor("wmT", [DMODEL, 512], BF16, kind=E_OUT),
    }

    s = Sched()
    emit_ffn(s, nc, dict(io, x=x0, wT=w0T, scinv=scinv, bias=b0, xo=x1),
             "ffn0")
    emit_mamba_a(s, nc, dict(io, x=x1, **ma))
    emit_maskprep(s, nc, dict(io, **mp))
    s.emit(nc)
    return nc


def build_launch_B():
    nc = bass.Bass("TRN2", target_bir_lowering=False)
    E_IN, E_OUT = "ExternalInput", "ExternalOutput"
    io = {}
    io["ident"] = nc.dram_tensor("ident", [128, 128], F32, kind=E_IN)
    io["ones_r"] = nc.dram_tensor("ones_r", [1, 128], F32, kind=E_IN)
    x1 = nc.dram_tensor("x1", [T, DMODEL], F32, kind=E_IN)
    mb = {
        "xnT_h": nc.dram_tensor("xnT_h", [DMODEL, T], BF16, kind=E_IN),
        "wqT": nc.dram_tensor("wqT", [DMODEL, DMODEL], F32, kind=E_IN),
        "k_h": nc.dram_tensor("k_h", [T, 1024], BF16, kind=E_IN),
        "v_h": nc.dram_tensor("v_h", [T, 1024], BF16, kind=E_IN),
        "dt_h": nc.dram_tensor("dt_h", [T, 32], F32, kind=E_IN),
        "G_h": nc.dram_tensor("G_h", [T, 32], F32, kind=E_IN),
        "gt_h": nc.dram_tensor("gt_h", [NCH * 32, 128], F32, kind=E_IN),
        "gtot_h": nc.dram_tensor("gtot_h", [1, 256], F32, kind=E_IN),
        "kdf_h": nc.dram_tensor("kdf_h", [T, 32], F32, kind=E_IN),
        "C_all": nc.dram_tensor("C_all", [8, 32, 128, 128], F32, kind=E_IN),
        "P_row": nc.dram_tensor("P_row", [1, 256], F32, kind=E_IN),
        "m_row": nc.dram_tensor("m_row", [1, 256], F32, kind=E_IN),
        "invm_row": nc.dram_tensor("invm_row", [1, 256], F32, kind=E_IN),
        "m_bcd": nc.dram_tensor("m_bcd", [128, 8], F32, kind=E_IN),
        "selh": nc.dram_tensor("selh", [32, 4096], F32, kind=E_IN),
        "tri": nc.dram_tensor("tri", [128, 128], F32, kind=E_IN),
        "gnorm": nc.dram_tensor("gnorm", [1, 128], F32, kind=E_IN),
        "cosT": nc.dram_tensor("cosT", [T, 128], F32, kind=E_IN),
        "sinLT": nc.dram_tensor("sinLT", [T, 128], F32, kind=E_IN),
    }
    yr_h = nc.dram_tensor("yr_h", [T, DMODEL], BF16)
    wgT = nc.dram_tensor("wgT", [DMODEL, DMODEL], F32, kind=E_IN)
    woT = nc.dram_tensor("woT", [DMODEL, DMODEL], F32, kind=E_IN)
    w0T = nc.dram_tensor("w0T", [DMODEL, DMODEL], F32, kind=E_IN)
    scinv = nc.dram_tensor("scinv", [1, 1024], F32, kind=E_IN)
    b0 = nc.dram_tensor("b0", [1, DMODEL], F32, kind=E_IN)
    qwT = nc.dram_tensor("qwT", [DMODEL, DMODEL], I32, kind=E_IN)
    qscale = nc.dram_tensor("qscale", [1, DMODEL], F32, kind=E_IN)
    b1 = nc.dram_tensor("b1", [1, DMODEL], F32, kind=E_IN)
    loraAT = nc.dram_tensor("loraAT", [DMODEL, 16], F32, kind=E_IN)
    loraBT = nc.dram_tensor("loraBT", [16, DMODEL], F32, kind=E_IN)
    wmT = nc.dram_tensor("wmT", [DMODEL, DMODEL], BF16, kind=E_IN)
    bm = nc.dram_tensor("bm", [1, DMODEL], F32, kind=E_IN)
    x2 = nc.dram_tensor("x2", [T, DMODEL], F32)
    x3 = nc.dram_tensor("x3", [T, DMODEL], F32)
    x4 = nc.dram_tensor("x4", [T, DMODEL], F32)
    x5 = nc.dram_tensor("x5", [T, DMODEL], F32)
    out = nc.dram_tensor("out", [T, DMODEL], F32, kind=E_OUT)

    s = Sched()
    emit_mamba_b(s, nc, dict(io, yr_h=yr_h, **mb))
    emit_mamba_c(s, nc, dict(io, x=x1, xnT_h=mb["xnT_h"], yr_h=yr_h,
                             wgT=wgT, woT=woT, xo=x2))
    emit_ffn(s, nc, dict(io, x=x2, wT=w0T, scinv=scinv, bias=b0, xo=x3),
             "ffn0")
    emit_ffn(s, nc, dict(io, x=x3, wT=qwT, qscale=qscale, bias=b1,
                         loraAT=loraAT, loraBT=loraBT, xo=x4), "ffn1")
    emit_ffn(s, nc, dict(io, x=x4, wT=wmT, bias=bm, xo=x5), "maskmm")
    emit_ffn(s, nc, dict(io, x=x5, wT=qwT, qscale=qscale, bias=b1,
                         loraAT=loraAT, loraBT=loraBT, xo=out), "ffn1")
    s.emit(nc)
    return nc


def kernel(**inputs):
    c = _prep_common(inputs)
    x0 = np.ascontiguousarray(
        inputs["L0_attn_hidden_states"].reshape(SEQ, DMODEL))
    w0T = np.ascontiguousarray(inputs["L0_ffn_weight"].T)
    scinv = np.ascontiguousarray(
        inputs["L0_ffn_weight_scale_inv"].T).reshape(1, 1024)
    b0 = np.ascontiguousarray(inputs["L0_ffn_bias"].reshape(1, DMODEL))
    wkT = np.ascontiguousarray(inputs["L0_attn_Wk"].T)
    wvT = np.ascontiguousarray(inputs["L0_attn_Wv"].T)
    winT = np.ascontiguousarray(inputs["L0_attn_Win"].T)
    wTm = np.ascontiguousarray(inputs["L1_attn_weight"].T)
    mask = inputs["L1_attn_mask"]

    nc_a = build_launch_A()
    ins_a = []
    for r in range(NCORES):
        sl = slice(r * T, (r + 1) * T)
        ins_a.append({
            "ident": c["ident"], "ones_r": c["ones_r"],
            "x0": np.ascontiguousarray(x0[sl]),
            "w0T": w0T, "scinv": scinv, "b0": b0,
            "wkT": wkT, "wvT": wvT, "winT": winT,
            "dtb": np.ascontiguousarray(
                inputs["L0_attn_dt_bias"].reshape(1, 32)),
            "alog": np.ascontiguousarray(
                inputs["L0_attn_A_log_bias"].reshape(1, 32)),
            "cosT": np.ascontiguousarray(c["cos"][sl]),
            "sinLT": np.ascontiguousarray(c["sinL"][sl]),
            "masksl": np.ascontiguousarray(mask[r * 512:(r + 1) * 512]),
            "wTsl": np.ascontiguousarray(wTm[:, r * 512:(r + 1) * 512]),
            "sel32": c["sel32"], "rep4": c["rep4"],
        })
    res_a = _run(nc_a, ins_a)

    C_all = np.ascontiguousarray(np.stack([r["C_h"] for r in res_a], 0))
    P_row = np.ascontiguousarray(
        np.stack([r["P_h"][0] for r in res_a], 0).reshape(1, 256))
    wmT_full = np.ascontiguousarray(
        np.concatenate([r["wmT"] for r in res_a], 1))

    nc_b = build_launch_B()
    qwT = np.ascontiguousarray(inputs["L1_ffn_qweight"].T)
    qsc = np.ascontiguousarray(inputs["L1_ffn_qscale"].reshape(1, DMODEL))
    b1 = np.ascontiguousarray(inputs["L1_ffn_bias"].reshape(1, DMODEL))
    lAT = np.ascontiguousarray(inputs["L1_ffn_lora_A"].T)
    lBT = np.ascontiguousarray(inputs["L1_ffn_lora_B"].T)
    wqT = np.ascontiguousarray(inputs["L0_attn_Wq"].T)
    wgT = np.ascontiguousarray(inputs["L0_attn_Wg"].T)
    woT = np.ascontiguousarray(inputs["L0_attn_Wo"].T)
    bmm = np.ascontiguousarray(inputs["L1_attn_bias"].reshape(1, DMODEL))
    ins_b = []
    for r in range(NCORES):
        sl = slice(r * T, (r + 1) * T)
        mrow = np.zeros((1, 256), dtype=np.float32)
        mrow[0, :r * 32] = 1.0
        mbc = np.zeros((128, 8), dtype=np.float32)
        mbc[:, :r] = 1.0
        o = res_a[r]
        ins_b.append({
            "ident": c["ident"], "ones_r": c["ones_r"],
            "x1": o["x1"], "xnT_h": o["xnT_h"], "wqT": wqT,
            "k_h": o["k_h"], "v_h": o["v_h"], "dt_h": o["dt_h"],
            "G_h": o["G_h"], "gt_h": o["gt_h"], "gtot_h": o["gtot_h"],
            "kdf_h": o["kdf_h"], "C_all": C_all, "P_row": P_row,
            "m_row": mrow, "invm_row": np.ascontiguousarray(1.0 - mrow),
            "m_bcd": mbc, "selh": c["selh"], "tri": c["tri"],
            "gnorm": np.ascontiguousarray(
                inputs["L0_attn_gnorm_w"].reshape(1, 128)),
            "cosT": np.ascontiguousarray(c["cos"][sl]),
            "sinLT": np.ascontiguousarray(c["sinL"][sl]),
            "wgT": wgT, "woT": woT, "w0T": w0T, "scinv": scinv, "b0": b0,
            "qwT": qwT, "qscale": qsc, "b1": b1, "loraAT": lAT,
            "loraBT": lBT, "wmT": wmT_full, "bm": bmm,
        })
    res_b = _run(nc_b, ins_b)
    out = np.concatenate([r["out"] for r in res_b], 0)
    return out.reshape(1, SEQ, DMODEL).astype(np.float32)


# version 03
# speedup vs baseline: 1.1193x; 1.1193x over previous
import contextlib
import numpy as np
import ml_dtypes
import concourse.bass as bass
import concourse.mybir as mybir
from concourse.bass_utils import run_bass_kernel_spmd

F32 = mybir.dt.float32
BF16 = mybir.dt.bfloat16
I32 = mybir.dt.int32
AF = mybir.ActivationFunctionType
ALU = mybir.AluOpType
AXX = mybir.AxisListType.X

NCORES = 8
DMODEL = 4096
SEQ = 8192
T = SEQ // NCORES
NT = T // 128
NI = DMODEL // 128
H, KVH, D, CHUNK = 32, 8, 128, 128
NCH = T // CHUNK
LN_EPS = 1e-5
RMS_EPS = 1e-5
BF = ml_dtypes.bfloat16
_UID = [0]


def _stage_pfx():
    _UID[0] += 1
    return "s%d_" % _UID[0]


class Sched:

    ENGS = ("te", "vec", "sc", "gp", "sy")

    def __init__(self):
        self.ops = {k: [] for k in self.ENGS}
        self.count = {}

    def tok(self, sem):
        return (sem, self.count.get(sem, 0))

    def add(self, eng, fn, waits=(), inc=None, inc_n=1):
        if inc is not None:
            self.count[inc] = self.count.get(inc, 0) + inc_n
            tok = (inc, self.count[inc])
        else:
            tok = None
        self.ops[eng].append((fn, tuple(w for w in waits if w is not None),
                              inc, inc_n))
        return tok

    def dma(self, eng, fn, waits=()):
        return self.add(eng, fn, waits=waits, inc=eng + "_d", inc_n=16)

    def barrier_wait(self, eng, tok):
        self.add(eng, lambda e: e.nop(), waits=[tok])

    def barrier(self):
        t1, t2 = self.tok("sy_d"), self.tok("gp_d")
        for e in self.ENGS:
            self.add(e, lambda e_: e_.nop(), waits=[t1, t2], inc="bar")
        bt = self.tok("bar")
        for e in self.ENGS:
            self.add(e, lambda e_: e_.nop(), waits=[bt])

    def emit(self, nc):
        names = set()
        for lst in self.ops.values():
            for _, waits, inc, _ in lst:
                names.update(s0 for s0, _ in waits)
                if inc:
                    names.add(inc)
        with contextlib.ExitStack() as stack:
            sems = {n: stack.enter_context(nc.semaphore(n))
                    for n in sorted(names)}
            with nc.Block() as block:
                def mk(eng_key):
                    def _(eng):
                        for fn, waits, inc, inc_n in self.ops[eng_key]:
                            for (s0, v) in waits:
                                if v > 0:
                                    eng.wait_ge(sems[s0], v)
                            inst = fn(eng)
                            if inc is not None:
                                assert inst is not None, eng_key
                                inst.then_inc(sems[inc], inc_n)
                    return _
                block.tensor(mk("te"))
                block.vector(mk("vec"))
                block.scalar(mk("sc"))
                block.gpsimd(mk("gp"))
                block.sync(mk("sy"))


def emit_ln_to_xnT(s, bufs, x_hbm, pfx="ln"):
    xtile, stats, xnbf, xnT, identb, ptr, id_tok = bufs
    ssum = stats[:, 0:1]
    ssq = stats[:, 1:2]
    nmean = stats[:, 2:3]
    var = stats[:, 3:4]
    rstd = stats[:, 4:5]
    nbx = 1 if xtile[0] is xtile[1] else 2
    nbn = 1 if xnbf[0] is xnbf[1] else 2
    prev_users = [None, None]
    prev_sq_rd = None
    prev_rstd_rd = None
    prev_var_rd = None
    prev_tr = [None, None]
    evacs = []
    last = None
    for t in range(NT):
        p = t % nbx
        xt = xtile[p]
        ld = s.dma("sy", lambda e, xt=xt, t=t: e.dma_start(
            out=xt[:, :], in_=x_hbm[t * 128:(t + 1) * 128, :]),
            waits=[prev_users[p]])
        sm = s.add("vec", lambda e, xt=xt: e.tensor_reduce(
            ssum[:, :], xt[:, :], AXX, ALU.add), waits=[ld], inc=pfx + "v")
        qq = s.add("sc", lambda e, xt=xt, pn=t % nbn: e.activation(
            xnbf[pn][:, :], xt[:, :], AF.Square, accum_out=ssq[:, :]),
            waits=[ld, prev_sq_rd, prev_tr[t % nbn]], inc=pfx + "s")
        nm = s.add("vec", lambda e: e.tensor_scalar(
            nmean[:, :], ssum[:, :], -1.0 / DMODEL, None, ALU.mult),
            waits=[sm], inc=pfx + "v")
        m2 = s.add("vec", lambda e: e.tensor_tensor(
            var[:, :], nmean[:, :], nmean[:, :], ALU.mult),
            waits=[nm, prev_var_rd], inc=pfx + "v")
        vv = s.add("vec", lambda e: e.scalar_tensor_tensor(
            var[:, :], ssq[:, :], 1.0 / DMODEL, var[:, :], ALU.mult,
            ALU.subtract), waits=[m2, qq], inc=pfx + "v")
        prev_sq_rd = vv
        l1 = s.add("sc", lambda e: e.activation(
            rstd[:, :], var[:, :], AF.Ln, bias=np.float32(LN_EPS)),
            waits=[vv, prev_rstd_rd], inc=pfx + "s")
        prev_var_rd = l1
        l2 = s.add("sc", lambda e: e.activation(
            rstd[:, :], rstd[:, :], AF.Exp, scale=np.float32(-0.5)), waits=[l1],
            inc=pfx + "s")
        pn = t % nbn
        nrm = s.add("vec", lambda e, xt=xt, pn=pn: e.tensor_scalar(
            xnbf[pn][:, :], xt[:, :], nmean[:, :], rstd[:, :], ALU.add,
            ALU.mult), waits=[l2, prev_tr[pn]], inc=pfx + "v")
        prev_rstd_rd = nrm
        for i in range(NI):
            n = t * NI + i
            pb = n % 2
            tr = s.add("te", lambda e, i=i, pn=pn, pb=pb: e.transpose(
                ptr[pb][:, 0:128], xnbf[pn][:, i * 128:(i + 1) * 128],
                identb[:, :]),
                waits=[nrm, id_tok, evacs[n - 2] if n >= 2 else None],
                inc=pfx + "t")
            ev = s.add("vec", lambda e, i=i, t=t, pb=pb: e.tensor_copy(
                xnT[:, i * T + t * 128: i * T + (t + 1) * 128],
                ptr[pb][:, 0:128]), waits=[tr], inc=pfx + "e")
            evacs.append(ev)
            last = ev
        prev_users[p] = s.tok(pfx + "t")
        prev_tr[t % nbn] = s.tok(pfx + "t")
    return last


def emit_ffn(s, nc, io, kind):
    x = io["x"]
    ident = io["ident"]
    bias = io["bias"]
    ones_r = io["ones_r"]
    wT = io["wT"]
    if kind == "ffn0":
        scinv = io["scinv"]
    elif kind == "ffn1":
        qscale = io["qscale"]
        loraAT = io["loraAT"]
        loraBT = io["loraBT"]
    xo = io["xo"]

    half = 0.5 if kind != "maskmm" else 1.0

    with contextlib.ExitStack() as stk:
        _p = _stage_pfx()

        def sb(name, shape, dt):
            return stk.enter_context(nc.sbuf_tensor(_p + name, shape, dt))

        def ps(name, shape, dt):
            return stk.enter_context(nc.psum_tensor(_p + name, shape, dt))

        stats = sb("stats", [128, 8], F32)
        xnT = sb("xnT", [128, NI * T], BF16)
        identb = sb("identb", [128, 128], BF16)
        biasb = sb("biasb", [1, DMODEL], BF16)
        onesb = sb("onesb", [1, 128], BF16)
        onesf = sb("onesf", [1, 128], F32)
        scrow = sb("scrow", [1, 1024], F32)
        scb = sb("scb", [128, 1024], F32)
        qsrow = sb("qsrow", [1, 512], F32)
        qsb = sb("qsb", [128, DMODEL], F32)
        uT = sb("uT", [16, T], BF16)
        loraBb = sb("loraBb", [16, DMODEL], BF16)
        loraAb = sb("loraAb", [128, NI * 16], BF16)
        xold0 = sb("xold0", [128, 512], F32)
        xold1 = sb("xold1", [128, 512], F32)
        stage0 = sb("stage0", [128, 512], F32)
        stage1 = sb("stage1", [128, 512], F32)
        ptr0 = ps("ptr0", [128, 1024], BF16)
        ptr1 = ps("ptr1", [128, 1024], BF16)
        pm0 = ps("pm0", [128, 512], F32)
        pm1 = ps("pm1", [128, 512], F32)
        pm2 = ps("pm2", [128, 512], F32)
        pm3 = ps("pm3", [128, 512], F32)
        paux = ps("paux", [128, 512], F32)
        plora = ps("plora", [16, 512], F32)
        pmm = [pm0, pm1, pm2, pm3]
        xold = [xold0, xold1]
        stage = [stage0, stage1]

        id_tok = s.dma("gp", lambda e: e.dma_start(out=identb[:, :],
                                                   in_=ident[:, :]))
        bias_tok = s.dma("gp", lambda e: e.dma_start(out=biasb[:, :],
                                                     in_=bias[:, :]))
        ones_tok = s.dma("gp", lambda e: e.dma_start(out=onesb[:, :],
                                                     in_=ones_r[:, :]))
        onesf_tok = s.dma("sy", lambda e: e.dma_start(out=onesf[:, :],
                                                      in_=ones_r[:, :]))

        scb_tok = None
        qsb_tok = None
        loraB_tok = None
        loraA_tok = None
        prev_bc_ev = None
        if kind == "ffn0":
            c4 = s.dma("sy", lambda e: e.dma_start(out=scrow[:, :],
                                                   in_=scinv[:, :]))
            for k in range(2):
                mmb = s.add("te", lambda e, k=k: e.matmul(
                    paux[:, :], onesf[0:1, :], scrow[0:1, k * 512:(k + 1) * 512],
                    start=True, stop=True), waits=[c4, onesf_tok, prev_bc_ev],
                    inc="px")
                prev_bc_ev = s.add("vec", lambda e, k=k: e.tensor_copy(
                    scb[:, k * 512:(k + 1) * 512], paux[:, :]), waits=[mmb],
                    inc="pxe")
            scb_tok = prev_bc_ev
        if kind == "ffn1":
            for k in range(8):
                c4 = s.dma("sy", lambda e, k=k: e.dma_start(
                    out=qsrow[:, :], in_=qscale[0:1, k * 512:(k + 1) * 512]),
                    waits=[s.tok("px")])
                mmb = s.add("te", lambda e, k=k: e.matmul(
                    paux[:, :], onesf[0:1, :], qsrow[0:1, :],
                    start=True, stop=True), waits=[c4, onesf_tok, prev_bc_ev],
                    inc="px")
                prev_bc_ev = s.add("vec", lambda e, k=k: e.tensor_copy(
                    qsb[:, k * 512:(k + 1) * 512], paux[:, :]), waits=[mmb],
                    inc="pxe")
            qsb_tok = prev_bc_ev
            c6 = s.dma("gp", lambda e: e.dma_start(out=loraBb[:, :],
                                                   in_=loraBT[:, :]))
            loraB_tok = s.add("vec", lambda e: e.tensor_scalar(
                loraBb[:, :], loraBb[:, :], 2.0, None, ALU.mult), waits=[c6],
                inc="px2")
            loraA_tok = s.dma("gp", lambda e: e.dma_start(
                out=loraAb.ap().rearrange("p (i r) -> p i r", r=16),
                in_=loraAT.ap().rearrange("(i p) r -> p i r", p=128)))

        cm1 = nc.sbuf_tensor(_p + "xtile0", [128, DMODEL], F32)
        cm2 = nc.sbuf_tensor(_p + "xtile1", [128, DMODEL], F32)
        cm3 = nc.sbuf_tensor(_p + "xnbf0", [128, DMODEL], BF16)
        cm4 = nc.sbuf_tensor(_p + "xnbf1", [128, DMODEL], BF16)
        xtile = [cm1.__enter__(), cm2.__enter__()]
        xnbf = [cm3.__enter__(), cm4.__enter__()]
        ln_done = emit_ln_to_xnT(
            s, (xtile, stats, xnbf, xnT, identb, [ptr0, ptr1], id_tok), x)
        for cm in (cm4, cm3, cm2, cm1):
            cm.__exit__(None, None, None)
        slabs = [sb("slabA", [128, NI * 512], BF16),
                 sb("slabB", [128, NI * 512], BF16)]

        lora_done = None
        if kind == "ffn1":
            prev_u_ev = None
            for tch in range(2):
                mm_tok = None
                for i in range(NI):
                    mm_tok = s.add("te", lambda e, i=i, tch=tch: e.matmul(
                        plora[:, :],
                        loraAb[:, i * 16:(i + 1) * 16],
                        xnT[:, i * T + tch * 512: i * T + (tch + 1) * 512],
                        start=(i == 0), stop=(i == NI - 1)),
                        waits=[ln_done, loraA_tok, prev_u_ev]
                        if i == 0 else (), inc="px2" if i == NI - 1 else None)
                prev_u_ev = s.add("vec", lambda e, tch=tch: e.tensor_copy(
                    uT[:, tch * 512:(tch + 1) * 512], plora[:, :]),
                    waits=[mm_tok], inc="pxe2")
            lora_done = prev_u_ev

        evac_tok_bank = [None] * 4
        slab_free = [ln_done, ln_done]
        store_tok = [None, None]
        evac_read_xold = [None, None]
        for oc in range(8):
            slab = slabs[oc % 2]
            dq_eng = "gp" if kind != "maskmm" else "sy"
            tk = s.dma(dq_eng, lambda e, oc=oc, slab=slab: e.dma_start(
                out=slab.ap().rearrange("p (g o) -> p g o", o=512),
                in_=wT[:, oc * 512:(oc + 1) * 512]
                .rearrange("(g p) o -> p g o", p=128)),
                waits=[slab_free[oc % 2]])
            if kind == "ffn0":
                for itile in range(NI):
                    for ob in range(4):
                        col = itile * 32 + oc * 4 + ob
                        off = itile * 512 + ob * 128
                        eng = ("vec", "sc", "gp")[(itile * 4 + ob) % 3]
                        if eng == "sc":
                            s.add("sc", lambda e, off=off, col=col, slab=slab:
                                  e.activation(
                                      slab[:, off:off + 128],
                                      slab[:, off:off + 128], AF.Copy,
                                      scale=scb[:, col:col + 1]),
                                  waits=[tk, scb_tok], inc="dq")
                        else:
                            s.add(eng, lambda e, off=off, col=col, slab=slab:
                                  e.tensor_scalar(
                                      slab[:, off:off + 128],
                                      slab[:, off:off + 128],
                                      scb[:, col:col + 1], None, ALU.mult),
                                  waits=[tk, scb_tok], inc="dq")
                tk = s.tok("dq")
            elif kind == "ffn1":
                for g in range(NI):
                    eng = ("vec", "gp")[g % 2]
                    off = g * 512
                    s.add(eng, lambda e, off=off, oc=oc, slab=slab:
                          e.tensor_tensor(
                              slab[:, off:off + 512], slab[:, off:off + 512],
                              qsb[:, oc * 512:(oc + 1) * 512], ALU.mult),
                          waits=[tk, qsb_tok], inc="dq")
                tk = s.tok("dq")
            slab_toks = [tk]
            for t in range(NT):
                it = oc * NT + t
                bank = t % 4
                grp_w = [ln_done, evac_tok_bank[bank], bias_tok, ones_tok] \
                    + slab_toks
                if kind == "ffn1":
                    grp_w += [lora_done, loraB_tok]
                s.add("te", lambda e, oc=oc, bank=bank: e.matmul(
                    pmm[bank][:, :], onesb[0:1, :],
                    biasb[0:1, oc * 512:(oc + 1) * 512],
                    start=True, stop=False), waits=grp_w)
                for i in range(NI):
                    is_last = (i == NI - 1) and kind != "ffn1"
                    s.add("te", lambda e, oc=oc, i=i, t=t, bank=bank,
                          slab=slab: e.matmul(
                              pmm[bank][:, :],
                              xnT[:, i * T + t * 128: i * T + (t + 1) * 128],
                              slab[:, i * 512:(i + 1) * 512],
                              start=False, stop=is_last),
                          inc="mm" if is_last else None)
                if kind == "ffn1":
                    s.add("te", lambda e, oc=oc, t=t, bank=bank: e.matmul(
                        pmm[bank][:, :], uT[:, t * 128:(t + 1) * 128],
                        loraBb[:, oc * 512:(oc + 1) * 512],
                        start=False, stop=True), inc="mm")
                mm_tok = s.tok("mm")
                p = it % 2
                xl = s.dma("sy", lambda e, oc=oc, t=t, p=p: e.dma_start(
                    out=xold[p][:, :],
                    in_=x[t * 128:(t + 1) * 128, oc * 512:(oc + 1) * 512]),
                    waits=[evac_read_xold[p]])
                ev = s.add("vec", lambda e, p=p, bank=bank:
                           e.scalar_tensor_tensor(
                               stage[p][:, :], pmm[bank][:, :], half,
                               xold[p][:, :], ALU.mult, ALU.add),
                           waits=[mm_tok, xl, store_tok[p]], inc="ev")
                evac_tok_bank[bank] = ev
                evac_read_xold[p] = ev
                store_tok[p] = s.dma("sy", lambda e, oc=oc, t=t, p=p:
                                     e.dma_start(
                                         out=xo[t * 128:(t + 1) * 128,
                                                oc * 512:(oc + 1) * 512],
                                         in_=stage[p][:, :]), waits=[ev])
            slab_free[oc % 2] = s.tok("mm")

    s.barrier()


def _prep_common(inputs):
    c = {}
    c["ident"] = np.eye(128, dtype=np.float32)
    c["ones_r"] = np.ones((1, 128), dtype=np.float32)
    sel32 = np.zeros((128, 4), dtype=np.float32)
    for p in range(128):
        sel32[p, p // 32] = 1.0
    c["sel32"] = sel32
    c["rep4"] = np.ascontiguousarray(sel32.T)
    idx = np.arange(128)
    c["tri"] = (idx[None, :] >= idx[:, None]).astype(np.float32)
    selh = np.zeros((32, 4096), dtype=np.float32)
    for h in range(32):
        selh[h, h * 128:(h + 1) * 128] = 1.0
    c["selh"] = selh
    inv_freq = 1.0 / 10000.0 ** (np.arange(0, 128, 2, dtype=np.float64) / 128)
    ang = np.arange(SEQ, dtype=np.float64)[:, None] * inv_freq
    cos = np.concatenate([np.cos(ang), np.cos(ang)], -1)
    sin = np.concatenate([np.sin(ang), np.sin(ang)], -1)
    sinL = sin.copy()
    sinL[:, :64] *= -1.0
    c["cos"] = cos.astype(np.float32)
    c["sinL"] = sinL.astype(np.float32)
    return c


def _run(nc, in_maps):
    res = run_bass_kernel_spmd(nc, in_maps, core_ids=list(range(NCORES)))
    return res.results


def emit_maskprep(s, nc, io):
    masksl = io["masksl"]
    wTsl = io["wTsl"]
    ident = io["ident"]
    sel32 = io["sel32"]
    rep4 = io["rep4"]
    wmT = io["wmT"]
    with contextlib.ExitStack() as stk:
        _p = _stage_pfx()

        def sb(name, shape, dt):
            return stk.enter_context(nc.sbuf_tensor(_p + name, shape, dt))

        def ps(name, shape, dt):
            return stk.enter_context(nc.psum_tensor(_p + name, shape, dt))

        mrow = sb("mrow", [128, DMODEL], F32)
        identf = sb("identf", [128, 128], F32)
        sel32f = sb("sel32f", [128, 4], F32)
        rep4f = sb("rep4f", [4, 128], F32)
        bm1 = sb("bm1", [4, 4 * DMODEL], F32)
        t1 = sb("t1", [128, 16], F32)
        bm2 = sb("bm2", [4, 32 * 16], F32)
        etmp = sb("etmp", [128, 16], F32)
        wt0 = sb("wt0", [128, 512], BF16)
        wt1 = sb("wt1", [128, 512], BF16)
        pa = ps("pa", [128, 512], F32)
        pb = ps("pb", [128, 512], F32)

        c1 = s.dma("sy", lambda e: e.dma_start(out=identf[:, :], in_=ident[:, :]))
        c2 = s.dma("sy", lambda e: e.dma_start(out=sel32f[:, :], in_=sel32[:, :]))
        c3 = s.dma("sy", lambda e: e.dma_start(out=rep4f[:, :], in_=rep4[:, :]))

        prev_ev = None
        prev_mrow_user = None
        for a in range(4):
            ld = s.dma("gp", lambda e, a=a: e.dma_start(
                out=mrow[:, :], in_=masksl[a * 128:(a + 1) * 128, :]),
                waits=[prev_mrow_user])
            for kc in range(8):
                mm = s.add("te", lambda e, kc=kc: e.matmul(
                    pa[0:4, :], sel32f[:, :],
                    mrow[:, kc * 512:(kc + 1) * 512], start=True, stop=True),
                    waits=[ld, c2, prev_ev], inc="t")
                prev_ev = s.add("vec", lambda e, a=a, kc=kc: e.tensor_scalar(
                    bm1[0:4, a * DMODEL + kc * 512: a * DMODEL + (kc + 1) * 512],
                    pa[0:4, :], 1.0, None, ALU.min), waits=[mm], inc="v")
            prev_mrow_user = prev_ev
        bm1_tok = prev_ev

        prev_pa_rd = None
        for kt in range(32):
            trs = None
            for a in range(4):
                trs = s.add("te", lambda e, kt=kt, a=a: e.transpose(
                    pb[:, a * 4:a * 4 + 4],
                    bm1[0:4, a * DMODEL + kt * 128: a * DMODEL + (kt + 1) * 128],
                    identf[0:4, 0:4]),
                    waits=[bm1_tok, c1] + ([s.tok("v")] if a == 0 else []),
                    inc="t")
            ev = s.add("vec", lambda e: e.tensor_copy(t1[:, :], pb[:, 0:16]),
                       waits=[trs], inc="v")
            mm = s.add("te", lambda e: e.matmul(
                pa[0:4, 0:16], sel32f[:, :], t1[:, :], start=True, stop=True),
                waits=[ev, prev_pa_rd], inc="t")
            prev_pa_rd = s.add("vec", lambda e, kt=kt: e.tensor_scalar(
                bm2[0:4, kt * 16:(kt + 1) * 16], pa[0:4, 0:16], 1.0, None,
                ALU.min), waits=[mm], inc="v")
        bm2_tok = prev_pa_rd

        prev_etmp_rd = None
        prev_pb_rd = None
        prev_store = [None, None]
        for kt in range(32):
            mm = s.add("te", lambda e, kt=kt: e.matmul(
                pb[:, 0:16], rep4f[:, :], bm2[0:4, kt * 16:(kt + 1) * 16],
                start=True, stop=True), waits=[bm2_tok, c3, prev_pb_rd],
                inc="t")
            ev = s.add("vec", lambda e: e.tensor_copy(etmp[:, :], pb[:, 0:16]),
                       waits=[mm, prev_etmp_rd], inc="v")
            prev_pb_rd = ev
            p = kt % 2
            wt = [wt0, wt1][p]
            ld = s.dma("gp", lambda e, kt=kt, wt=wt: e.dma_start(
                out=wt[:, :], in_=wTsl[kt * 128:(kt + 1) * 128, :]),
                waits=[prev_store[p]])
            apl = None
            for go in range(16):
                apl = s.add("vec", lambda e, go=go, wt=wt: e.tensor_scalar(
                    wt[:, go * 32:(go + 1) * 32], wt[:, go * 32:(go + 1) * 32],
                    etmp[:, go:go + 1], None, ALU.mult),
                    waits=[ev, ld], inc="v")
            prev_etmp_rd = apl
            prev_store[p] = s.dma("sy", lambda e, kt=kt, wt=wt: e.dma_start(
                out=wmT[kt * 128:(kt + 1) * 128, :], in_=wt[:, :]), waits=[apl])

    s.barrier()


def emit_mamba_a(s, nc, io):
    x = io["x"]
    wkT = io["wkT"]
    wvT = io["wvT"]
    winT = io["winT"]
    dtb = io["dtb"]
    alog = io["alog"]
    cosT = io["cosT"]
    sinLT = io["sinLT"]
    ident = io["ident"]
    ones_r = io["ones_r"]
    xnT_h = io["xnT_h"]
    k_h = io["k_h"]
    v_h = io["v_h"]
    dt_h = io["dt_h"]
    G_h = io["G_h"]
    gt_h = io["gt_h"]
    gtot_h = io["gtot_h"]
    kdf_h = io["kdf_h"]
    C_h = io["C_h"]
    P_h = io["P_h"]
    with contextlib.ExitStack() as stk:
        _p = _stage_pfx()

        def sb(name, shape, dt):
            return stk.enter_context(nc.sbuf_tensor(_p + name, shape, dt))

        def ps(name, shape, dt):
            return stk.enter_context(nc.psum_tensor(_p + name, shape, dt))

        xtile0 = sb("xtile0", [128, DMODEL], F32)
        xtile = [xtile0, xtile0]
        stats = sb("stats", [128, 8], F32)
        xnbf0 = sb("xnbf0", [128, DMODEL], BF16)
        xnbf = [xnbf0, xnbf0]
        xnT = sb("xnT", [128, NI * T], BF16)
        identb = sb("identb", [128, 128], BF16)
        identf = sb("identf", [128, 128], F32)
        onesf = sb("onesf", [1, 128], F32)
        slab = sb("slab", [128, NI * 512], BF16)
        winb = sb("winb", [128, NI * 32], BF16)
        cos_sb = sb("cos_sb", [128, NT * 128], F32)
        sin_sb = sb("sin_sb", [128, NT * 128], F32)
        k_sb = sb("k_sb", [128, NT * 1024], BF16)
        v_sb = sb("v_sb", [128, NT * 1024], BF16)
        tmpA = sb("tmpA", [128, 128], F32)
        tmpB = sb("tmpB", [128, 128], F32)
        z_sb = sb("z_sb", [128, 256], F32)
        za_sb = sb("za_sb", [128, 256], F32)
        dt_sb = sb("dt_sb", [128, 256], F32)
        g_sb = sb("g_sb", [128, 256], F32)
        G_sb = sb("G_sb", [128, 256], F32)
        gts = sb("gts", [32, NCH * 128], F32)
        gtot_row = sb("gtot_row", [1, 256], F32)
        gsum_row = sb("gsum_row", [1, 32], F32)
        p_row = sb("p_row", [1, 32], F32)
        gtotb = sb("gtotb", [128, 32], F32)
        cdb = sb("cdb", [128, 256], F32)
        kdf = sb("kdf", [128, 256], F32)
        dtbb = sb("dtbb", [128, 32], F32)
        ab = sb("ab", [128, 32], F32)
        dtb_row = sb("dtb_row", [1, 32], F32)
        a_row = sb("a_row", [1, 32], F32)
        kdec = sb("kdec", [128, 128], BF16)
        xc = sb("xc", [128, 128], BF16)
        S = sb("S", [128, 128], F32)
        C_sb = sb("C_sb", [128, 32 * 128], F32)
        ptr = [ps("ptr0", [128, 1024], BF16), ps("ptr1", [128, 1024], BF16)]
        pmm = [ps("pm0", [128, 512], F32), ps("pm1", [128, 512], F32),
               ps("pm2", [128, 512], F32), ps("pm3", [128, 512], F32)]
        paux = ps("paux", [128, 512], F32)
        pcs = ps("pcs", [128, 512], F32)

        id_tok = s.dma("gp", lambda e: e.dma_start(out=identb[:, :],
                                                   in_=ident[:, :]))
        idf_tok = s.dma("sy", lambda e: e.dma_start(out=identf[:, :],
                                                    in_=ident[:, :]))
        onesf_tok = s.dma("sy", lambda e: e.dma_start(out=onesf[:, :],
                                                      in_=ones_r[:, :]))
        cs_tok = s.dma("sy", lambda e: e.dma_start(
            out=cos_sb.ap().rearrange("p (t d) -> p t d", d=128),
            in_=cosT.ap().rearrange("(t p) d -> p t d", p=128)))
        sn_tok = s.dma("sy", lambda e: e.dma_start(
            out=sin_sb.ap().rearrange("p (t d) -> p t d", d=128),
            in_=sinLT.ap().rearrange("(t p) d -> p t d", p=128)))
        dtb_tok = s.dma("sy", lambda e: e.dma_start(out=dtb_row[:, :],
                                                    in_=dtb[:, :]))
        a_tok = s.dma("sy", lambda e: e.dma_start(out=a_row[:, :],
                                                  in_=alog[:, :]))
        win_tok = s.dma("gp", lambda e: e.dma_start(
            out=winb.ap().rearrange("p (i h) -> p i h", h=32),
            in_=winT.ap().rearrange("(i p) h -> p i h", p=128)))

        ln_done = emit_ln_to_xnT(
            s, (xtile, stats, xnbf, xnT, identb, ptr, id_tok), x)
        s.dma("sy", lambda e: e.dma_start(
            out=xnT_h.ap().rearrange("(i p) t -> p i t", p=128),
            in_=xnT.ap().rearrange("p (i t) -> p i t", t=T)), waits=[ln_done])

        def proj_1024(wT_t, out_sb, name, slab_free=None):
            evac_tok_bank = [None] * 4
            out_toks = []
            for oc in range(2):
                tk = s.dma("gp", lambda e, oc=oc, wT_t=wT_t: e.dma_start(
                    out=slab.ap().rearrange("p (i o) -> p i o", o=512),
                    in_=wT_t[:, oc * 512:(oc + 1) * 512]
                    .rearrange("(i p) o -> p i o", p=128)), waits=[slab_free])
                for t in range(NT):
                    bank = t % 4
                    s.add("te", lambda e, t=t, bank=bank: e.matmul(
                        pmm[bank][:, :],
                        xnT[:, 0 * T + t * 128: 0 * T + (t + 1) * 128],
                        slab[:, 0:512], start=True, stop=False),
                        waits=[ln_done, tk, evac_tok_bank[bank]])
                    for i in range(1, NI):
                        s.add("te", lambda e, i=i, t=t, bank=bank: e.matmul(
                            pmm[bank][:, :],
                            xnT[:, i * T + t * 128: i * T + (t + 1) * 128],
                            slab[:, i * 512:(i + 1) * 512],
                            start=False, stop=(i == NI - 1)),
                            inc="mm" if i == NI - 1 else None)
                    mm_tok = s.tok("mm")
                    ev = s.add("vec", lambda e, t=t, oc=oc, bank=bank, o=out_sb:
                               e.tensor_copy(
                                   o[:, t * 1024 + oc * 512:
                                     t * 1024 + oc * 512 + 512],
                                   pmm[bank][:, :]), waits=[mm_tok],
                               inc="ev")
                    evac_tok_bank[bank] = ev
                    out_toks.append(ev)
                slab_free = s.tok("mm")
            return s.tok("ev"), slab_free

        k_done, kv_slab_free = proj_1024(wkT, k_sb, "k")
        v_done, v_slab_free = proj_1024(wvT, v_sb, "v",
                                        slab_free=kv_slab_free)

        dt_mm_toks = []
        for t in range(NT):
            s.add("te", lambda e, t=t: e.matmul(
                pcs[:, t * 32:(t + 1) * 32],
                xnT[:, 0 * T + t * 128: 0 * T + (t + 1) * 128],
                winb[:, 0:32], start=True, stop=False),
                waits=[ln_done, win_tok] if t == 0 else [])
            for i in range(1, NI):
                s.add("te", lambda e, i=i, t=t: e.matmul(
                    pcs[:, t * 32:(t + 1) * 32],
                    xnT[:, i * T + t * 128: i * T + (t + 1) * 128],
                    winb[:, i * 32:(i + 1) * 32],
                    start=False, stop=(i == NI - 1)),
                    inc="mm" if i == NI - 1 else None)
            dt_mm_toks.append(s.tok("mm"))

        rope_done = None
        for t in range(NT):
            for kv in range(KVH):
                off = t * 1024 + kv * 128
                co = t * 128
                m1 = s.add("vec", lambda e, off=off, co=co: e.tensor_tensor(
                    tmpA[:, :], k_sb[:, off:off + 128], cos_sb[:, co:co + 128],
                    ALU.mult), waits=[k_done, cs_tok, rope_done], inc="rp")
                m2 = s.add("vec", lambda e, off=off, co=co: e.tensor_tensor(
                    tmpB[:, 0:64], k_sb[:, off + 64:off + 128],
                    sin_sb[:, co:co + 64], ALU.mult), waits=[sn_tok], inc="rp")
                m3 = s.add("vec", lambda e, off=off, co=co: e.tensor_tensor(
                    tmpB[:, 64:128], k_sb[:, off:off + 64],
                    sin_sb[:, co + 64:co + 128], ALU.mult), inc="rp")
                rope_done = s.add("vec", lambda e, off=off: e.tensor_tensor(
                    k_sb[:, off:off + 128], tmpA[:, :], tmpB[:, :], ALU.add),
                    inc="rp")
        k_out_tok = s.dma("sy", lambda e: e.dma_start(
            out=k_h.ap().rearrange("(t p) o -> p t o", p=128),
            in_=k_sb.ap().rearrange("p (t o) -> p t o", o=1024)),
            waits=[rope_done])
        v_out_tok = s.dma("sy", lambda e: e.dma_start(
            out=v_h.ap().rearrange("(t p) o -> p t o", p=128),
            in_=v_sb.ap().rearrange("p (t o) -> p t o", o=1024)),
            waits=[v_done])

        bc1 = s.add("te", lambda e: e.matmul(
            paux[:, 0:32], onesf[0:1, :], dtb_row[0:1, :], start=True,
            stop=True), waits=[dtb_tok, onesf_tok], inc="mm")
        dtbb_tok = s.add("vec", lambda e: e.tensor_copy(dtbb[:, :],
                                                        paux[:, 0:32]),
                         waits=[bc1], inc="ev")
        ae = s.add("sc", lambda e: e.activation(a_row[:, :], a_row[:, :],
                                                AF.Exp), waits=[a_tok],
                   inc="sc")
        an = s.add("vec", lambda e: e.tensor_scalar(
            a_row[:, :], a_row[:, :], -1.0, None, ALU.mult), waits=[ae],
            inc="ev")
        bc2 = s.add("te", lambda e: e.matmul(
            paux[:, 32:64], onesf[0:1, :], a_row[0:1, :], start=True,
            stop=True), waits=[an, onesf_tok, dtbb_tok], inc="mm")
        ab_tok = s.add("vec", lambda e: e.tensor_copy(ab[:, :],
                                                      paux[:, 32:64]),
                       waits=[bc2], inc="ev")

        z_toks = []
        for t in range(NT):
            z_toks.append(s.add("vec", lambda e, t=t: e.tensor_tensor(
                z_sb[:, t * 32:(t + 1) * 32], pcs[:, t * 32:(t + 1) * 32],
                dtbb[:, :], ALU.add), waits=[dt_mm_toks[-1], dtbb_tok],
                inc="ev"))
        sab = s.add("sc", lambda e: e.activation(
            za_sb[:, :], z_sb[:, :], AF.Abs), waits=z_toks, inc="sc")
        sex = s.add("sc", lambda e: e.activation(
            za_sb[:, :], za_sb[:, :], AF.Exp, scale=np.float32(-1.0)),
            waits=[sab], inc="sc")
        sln = s.add("sc", lambda e: e.activation(
            za_sb[:, :], za_sb[:, :], AF.Ln, bias=np.float32(1.0)),
            waits=[sex], inc="sc")
        srl = s.add("vec", lambda e: e.tensor_scalar(
            dt_sb[:, :], z_sb[:, :], 0.0, None, ALU.max), waits=[sln],
            inc="ev")
        sdt = s.add("vec", lambda e: e.tensor_tensor(
            dt_sb[:, :], dt_sb[:, :], za_sb[:, :], ALU.add), waits=[srl],
            inc="ev")
        g_toks = []
        for t in range(NT):
            g_toks.append(s.add("vec", lambda e, t=t: e.tensor_tensor(
                g_sb[:, t * 32:(t + 1) * 32], dt_sb[:, t * 32:(t + 1) * 32],
                ab[:, :], ALU.mult), waits=[sdt, ab_tok], inc="ev"))
        dt_out = s.dma("sy", lambda e: e.dma_start(
            out=dt_h.ap().rearrange("(t p) h -> p t h", p=128),
            in_=dt_sb.ap().rearrange("p (t h) -> p t h", h=32)), waits=[sdt])

        gsum_init = s.add("vec", lambda e: e.memset(gsum_row[:, :], 0.0),
                          inc="ev")
        prev_paux = [ab_tok]
        chunk_toks = []
        for c in range(NCH):
            tr1 = s.add("te", lambda e, c=c: e.transpose(
                paux[0:32, 0:128], g_sb[:, c * 32:(c + 1) * 32],
                identf[:, :]), waits=[g_toks[c], idf_tok] + prev_paux,
                inc="mm")
            scn = s.add("vec", lambda e, c=c: e.tensor_tensor_scan(
                gts[:, c * 128:(c + 1) * 128], paux[0:32, 0:128],
                identf[0:32, 0:128], 0.0, ALU.add, ALU.bypass),
                waits=[tr1], inc="ev")
            tr2 = s.add("te", lambda e, c=c: e.transpose(
                paux[:, 128:160], gts[:, c * 128:(c + 1) * 128]
                if False else gts[0:32, c * 128:(c + 1) * 128],
                identf[0:32, 0:32]), waits=[scn], inc="mm")
            gev = s.add("vec", lambda e, c=c: e.tensor_copy(
                G_sb[:, c * 32:(c + 1) * 32], paux[:, 128:160]),
                waits=[tr2], inc="ev")
            tr3 = s.add("te", lambda e, c=c: e.matmul(
                paux[0:1, 160:192], gts[0:32, c * 128 + 127: c * 128 + 128],
                identf[0:32, 0:32], start=True, stop=True),
                waits=[scn, gev], inc="mm")
            gtv = s.add("vec", lambda e, c=c: e.tensor_copy(
                gtot_row[0:1, c * 32:(c + 1) * 32], paux[0:1, 160:192]),
                waits=[tr3], inc="ev")
            gsm = s.add("vec", lambda e, c=c: e.tensor_tensor(
                gsum_row[:, :], gsum_row[:, :],
                gtot_row[0:1, c * 32:(c + 1) * 32], ALU.add),
                waits=[gtv, gsum_init], inc="ev")
            bcg = s.add("te", lambda e, c=c: e.matmul(
                paux[:, 192:224], onesf[0:1, :],
                gtot_row[0:1, c * 32:(c + 1) * 32], start=True, stop=True),
                waits=[gtv, onesf_tok], inc="mm")
            gbv = s.add("vec", lambda e: e.tensor_copy(gtotb[:, :],
                                                       paux[:, 192:224]),
                        waits=[bcg], inc="ev")
            cdv = s.add("sc", lambda e, c=c: e.activation(
                cdb[:, c * 32:(c + 1) * 32], gtotb[:, :], AF.Exp),
                waits=[gbv], inc="sc")
            kdd = s.add("vec", lambda e, c=c: e.tensor_tensor(
                kdf[:, c * 32:(c + 1) * 32], gtotb[:, :],
                G_sb[:, c * 32:(c + 1) * 32], ALU.subtract),
                waits=[gbv, gev], inc="ev")
            kde = s.add("sc", lambda e, c=c: e.activation(
                kdf[:, c * 32:(c + 1) * 32], kdf[:, c * 32:(c + 1) * 32],
                AF.Exp), waits=[kdd], inc="sc")
            prev_paux = [gev, gtv, gbv, kde, cdv]
            chunk_toks.append((kde, cdv, gev))
        psum_exp = s.add("sc", lambda e: e.activation(
            p_row[:, :], gsum_row[:, :], AF.Exp),
            waits=[s.tok("ev")], inc="sc")
        s.dma("sy", lambda e: e.dma_start(out=P_h[:, :], in_=p_row[:, :]),
              waits=[psum_exp])
        s.dma("sy", lambda e: e.dma_start(out=gtot_h[:, :],
                                          in_=gtot_row[:, :]),
              waits=[s.tok("ev")])
        s.dma("sy", lambda e: e.dma_start(
            out=G_h.ap().rearrange("(t p) h -> p t h", p=128),
            in_=G_sb.ap().rearrange("p (t h) -> p t h", h=32)),
            waits=[s.tok("ev")])
        s.dma("sy", lambda e: e.dma_start(
            out=kdf_h.ap().rearrange("(t p) h -> p t h", p=128),
            in_=kdf.ap().rearrange("p (t h) -> p t h", h=32)),
            waits=[s.tok("sc")])
        s.dma("sy", lambda e: e.dma_start(
            out=gt_h.ap().rearrange("(c g) t -> g c t", g=32),
            in_=gts.ap().rearrange("g (c t) -> g c t", t=128)),
            waits=[s.tok("ev")])

        all_prep = [s.tok("ev"), s.tok("sc"), rope_done]
        prev_cs_rd = None
        for h in range(H):
            kv = h // 4
            sini = s.add("vec", lambda e: e.memset(S[:, :], 0.0),
                         waits=[s.tok("cw") if h > 0 else None], inc="ev")
            for c in range(NCH):
                col = c * 32 + h
                koff = c * 1024 + kv * 128
                kd = s.add("gp", lambda e, koff=koff, col=col: e.tensor_scalar(
                    kdec[:, :], k_sb[:, koff:koff + 128],
                    kdf[:, col:col + 1], None, ALU.mult),
                    waits=all_prep + [prev_cs_rd], inc="gc")
                xv = s.add("gp", lambda e, koff=koff, col=col: e.tensor_scalar(
                    xc[:, :], v_sb[:, koff:koff + 128],
                    dt_sb[:, col:col + 1], None, ALU.mult), inc="gc")
                csm = s.add("te", lambda e: e.matmul(
                    pcs[:, 256:384], kdec[:, :], xc[:, :], start=True,
                    stop=True), waits=[kd, xv, prev_cs_rd], inc="mm")
                prev_cs_rd = s.add("vec", lambda e, col=col: e.scalar_tensor_tensor(
                    S[:, :], S[:, :], cdb[:, col:col + 1], pcs[:, 256:384],
                    ALU.mult, ALU.add), waits=[csm, sini], inc="ev")
            s.add("vec", lambda e, h=h: e.tensor_copy(
                C_sb[:, h * 128:(h + 1) * 128], S[:, :]),
                waits=[prev_cs_rd], inc="cw")
        s.dma("sy", lambda e: e.dma_start(
            out=C_h.ap().rearrange("h d e -> d h e"),
            in_=C_sb.ap().rearrange("d (h e) -> d h e", e=128)),
            waits=[s.tok("cw")])

    s.barrier()


def emit_mamba_b(s, nc, io, dbg=False):
    xnT_h = io["xnT_h"]
    wqT = io["wqT"]
    k_h = io["k_h"]
    v_h = io["v_h"]
    dt_h = io["dt_h"]
    G_h = io["G_h"]
    gt_h = io["gt_h"]
    gtot_h = io["gtot_h"]
    kdf_h = io["kdf_h"]
    C_all = io["C_all"]
    P_row = io["P_row"]
    m_row = io["m_row"]
    invm_row = io["invm_row"]
    m_bcd = io["m_bcd"]
    selh = io["selh"]
    tri = io["tri"]
    gnorm = io["gnorm"]
    cosT = io["cosT"]
    sinLT = io["sinLT"]
    ident = io["ident"]
    ones_r = io["ones_r"]
    yr_h = io["yr_h"]
    if dbg:
        dbg_h = io["dbg_h"]
    with contextlib.ExitStack() as stk:
        _p = _stage_pfx()

        def sb(name, shape, dt):
            return stk.enter_context(nc.sbuf_tensor(_p + name, shape, dt))

        def ps(name, shape, dt):
            return stk.enter_context(nc.psum_tensor(_p + name, shape, dt))

        q_sb = sb("q_sb", [128, NT * DMODEL], BF16)
        cos_sb = sb("cos_sb", [128, NT * 128], F32)
        sin_sb = sb("sin_sb", [128, NT * 128], F32)
        identb = sb("identb", [128, 128], BF16)
        identf = sb("identf", [128, 128], F32)
        onesf = sb("onesf", [1, 128], F32)
        tmpA = sb("tmpA", [128, 128], F32)
        tmpB = sb("tmpB", [128, 128], F32)
        rows = sb("rows", [1, 1024], F32)
        gtrow = sb("gtrow", [1, 256], F32)
        gnb = sb("gnb", [128, 128], F32)
        tri_sb = sb("tri_sb", [128, 128], F32)
        m_bc = sb("m_bc", [128, 8], F32)
        qtT = sb("qtT", [128, 128], BF16)
        kTt = sb("kTt", [128, 128], BF16)
        dec0 = sb("dec0", [128, 128], F32)
        dec_sb = sb("dec_sb", [128, 128], F32)
        dm = sb("dm", [128, 128], F32)
        st_sb = sb("st_sb", [128, 128], BF16)
        xc_sb = sb("xc_sb", [128, 128], BF16)
        kdec = sb("kdec", [128, 128], BF16)
        sc_sb = sb("sc_sb", [128, 128], BF16)
        yfull = sb("yfull", [128, 128], F32)
        zc_sb = sb("zc_sb", [128, 128], F32)
        y1_sb = sb("y1_sb", [128, 128], F32)
        tmpc = sb("tmpc", [128, 128], F32)
        selh_sb = sb("selh_sb", [32, 128], F32)
        ss = sb("ss", [128, 1], F32)
        rr = sb("rr", [128, 1], F32)
        dbg_sb = sb("dbg_sb", [128, 24 * 128], F32) if dbg else None
        pmm = [ps("pm0", [128, 512], F32), ps("pm1", [128, 512], F32),
               ps("pm2", [128, 512], F32), ps("pm3", [128, 512], F32)]
        pz = ps("pz", [128, 512], F32)
        ptr = [ps("ptr0", [128, 1024], BF16), ps("ptr1", [128, 1024], BF16)]
        paux = ps("paux", [128, 512], F32)
        pgb, psc, py, pcs = pmm[0], pmm[1], pmm[2], pmm[3]

        id_tok = s.dma("gp", lambda e: e.dma_start(out=identb[:, :],
                                                   in_=ident[:, :]))
        idf_tok = s.dma("sy", lambda e: e.dma_start(out=identf[:, :],
                                                    in_=ident[:, :]))
        onesf_tok = s.dma("sy", lambda e: e.dma_start(out=onesf[:, :],
                                                      in_=ones_r[:, :]))
        cs_tok = s.dma("sy", lambda e: e.dma_start(
            out=cos_sb.ap().rearrange("p (t d) -> p t d", d=128),
            in_=cosT.ap().rearrange("(t p) d -> p t d", p=128)))
        sn_tok = s.dma("sy", lambda e: e.dma_start(
            out=sin_sb.ap().rearrange("p (t d) -> p t d", d=128),
            in_=sinLT.ap().rearrange("(t p) d -> p t d", p=128)))
        tri_tok = s.dma("sy", lambda e: e.dma_start(out=tri_sb[:, :],
                                                    in_=tri[:, :]))
        mbc_tok = s.dma("sy", lambda e: e.dma_start(out=m_bc[:, :],
                                                    in_=m_bcd[:, :]))
        r1 = s.dma("sy", lambda e: e.dma_start(out=rows[0:1, 0:256],
                                               in_=P_row[:, :]))
        r2 = s.dma("sy", lambda e: e.dma_start(out=rows[0:1, 256:512],
                                               in_=m_row[:, :]))
        r3 = s.dma("sy", lambda e: e.dma_start(out=rows[0:1, 512:768],
                                               in_=invm_row[:, :]))
        r4 = s.dma("sy", lambda e: e.dma_start(out=gtrow[:, :],
                                               in_=gtot_h[:, :]))
        gn_tok = s.dma("sy", lambda e: e.dma_start(out=gnb[0:1, :],
                                                   in_=gnorm[:, :]))

        cm_slab = nc.sbuf_tensor(_p + "slab", [128, NI * 512], BF16)
        slab = cm_slab.__enter__()
        cm_xnT = nc.sbuf_tensor(_p + "xnT", [128, NI * T], BF16)
        xnT = cm_xnT.__enter__()
        xn_tok = s.dma("sy", lambda e: e.dma_start(
            out=xnT.ap().rearrange("p (i t) -> p i t", t=T),
            in_=xnT_h.ap().rearrange("(i p) t -> p i t", p=128)))
        evac_tok_bank = [None] * 4
        slab_free = None
        for oc in range(8):
            tk = s.dma("gp", lambda e, oc=oc: e.dma_start(
                out=slab.ap().rearrange("p (i o) -> p i o", o=512),
                in_=wqT[:, oc * 512:(oc + 1) * 512]
                .rearrange("(i p) o -> p i o", p=128)), waits=[slab_free])
            for t in range(NT):
                bank = t % 4
                s.add("te", lambda e, t=t, bank=bank: e.matmul(
                    pmm[bank][:, :],
                    xnT[:, 0 * T + t * 128: 0 * T + (t + 1) * 128],
                    slab[:, 0:512], start=True, stop=False),
                    waits=[xn_tok, tk, evac_tok_bank[bank]])
                for i in range(1, NI):
                    s.add("te", lambda e, i=i, t=t, bank=bank: e.matmul(
                        pmm[bank][:, :],
                        xnT[:, i * T + t * 128: i * T + (t + 1) * 128],
                        slab[:, i * 512:(i + 1) * 512],
                        start=False, stop=(i == NI - 1)),
                        inc="mm" if i == NI - 1 else None)
                mm_tok = s.tok("mm")
                ev = s.add("vec", lambda e, t=t, oc=oc, bank=bank:
                           e.tensor_copy(
                               q_sb[:, t * DMODEL + oc * 512:
                                    t * DMODEL + oc * 512 + 512],
                               pmm[bank][:, :]), waits=[mm_tok], inc="ev")
                evac_tok_bank[bank] = ev
            slab_free = s.tok("mm")
        q_done = s.tok("ev")
        qmm_done = s.tok("mm")
        cm_xnT.__exit__(None, None, None)
        cm_slab.__exit__(None, None, None)

        k_sb = sb("k_sb", [128, NT * 1024], BF16)
        v_sb = sb("v_sb", [128, NT * 1024], BF16)
        S_all = sb("S_all", [128, 32 * 128], F32)
        cslab = sb("cslab", [128, 32 * 128], F32)
        gts = sb("gts", [32, NCH * 128], F32)
        dt_sb = sb("dt_sb", [128, 256], F32)
        G_sb = sb("G_sb", [128, 256], F32)
        negG = sb("negG", [128, 256], F32)
        expG = sb("expG", [128, 256], F32)
        kdf = sb("kdf", [128, 256], F32)
        cdb = sb("cdb", [128, 256], F32)
        ptilb = sb("ptilb", [128, 256], F32)

        kld = s.dma("sy", lambda e: e.dma_start(
            out=k_sb.ap().rearrange("p (t o) -> p t o", o=1024),
            in_=k_h.ap().rearrange("(t p) o -> p t o", p=128)),
            waits=[qmm_done])
        vld = s.dma("sy", lambda e: e.dma_start(
            out=v_sb.ap().rearrange("p (t o) -> p t o", o=1024),
            in_=v_h.ap().rearrange("(t p) o -> p t o", p=128)),
            waits=[qmm_done])
        dld = s.dma("sy", lambda e: e.dma_start(
            out=dt_sb.ap().rearrange("p (t h) -> p t h", h=32),
            in_=dt_h.ap().rearrange("(t p) h -> p t h", p=128)),
            waits=[qmm_done])
        gld = s.dma("sy", lambda e: e.dma_start(
            out=G_sb.ap().rearrange("p (t h) -> p t h", h=32),
            in_=G_h.ap().rearrange("(t p) h -> p t h", p=128)),
            waits=[qmm_done])
        gtld = s.dma("sy", lambda e: e.dma_start(
            out=gts.ap().rearrange("g (c t) -> g c t", t=128),
            in_=gt_h.ap().rearrange("(c g) t -> g c t", g=32)),
            waits=[qmm_done])
        kdld = s.dma("sy", lambda e: e.dma_start(
            out=kdf.ap().rearrange("p (t h) -> p t h", h=32),
            in_=kdf_h.ap().rearrange("(t p) h -> p t h", p=128)),
            waits=[qmm_done])

        rope_done = None
        for t in range(NT):
            for h in range(H):
                off = t * DMODEL + h * 128
                co = t * 128
                s.add("vec", lambda e, off=off, co=co: e.tensor_tensor(
                    tmpA[:, :], q_sb[:, off:off + 128], cos_sb[:, co:co + 128],
                    ALU.mult), waits=[q_done, cs_tok, sn_tok, rope_done],
                    inc="rp")
                s.add("vec", lambda e, off=off, co=co: e.tensor_tensor(
                    tmpB[:, 0:64], q_sb[:, off + 64:off + 128],
                    sin_sb[:, co:co + 64], ALU.mult), inc="rp")
                s.add("vec", lambda e, off=off, co=co: e.tensor_tensor(
                    tmpB[:, 64:128], q_sb[:, off:off + 64],
                    sin_sb[:, co + 64:co + 128], ALU.mult), inc="rp")
                rope_done = s.add("vec", lambda e, off=off: e.tensor_tensor(
                    q_sb[:, off:off + 128], tmpA[:, :], tmpB[:, :], ALU.add),
                    inc="rp")

        ng_tok = s.add("vec", lambda e: e.tensor_scalar(
            negG[:, :], G_sb[:, :], -1.0, None, ALU.mult), waits=[gld],
            inc="ev")
        eg_tok = s.add("sc", lambda e: e.activation(
            expG[:, :], G_sb[:, :], AF.Exp), waits=[gld], inc="sc")
        bc1 = s.add("te", lambda e: e.matmul(
            paux[:, 0:256], onesf[0:1, :], gtrow[0:1, :], start=True,
            stop=True), waits=[r4, onesf_tok], inc="tt")
        cdv = s.add("vec", lambda e: e.tensor_copy(cdb[:, :], paux[:, 0:256]),
                    waits=[bc1], inc="ev")
        cde = s.add("sc", lambda e: e.activation(
            cdb[:, :], cdb[:, :], AF.Exp), waits=[cdv], inc="sc")
        bc2 = s.add("te", lambda e: e.matmul(
            paux[:, 256:384], onesf[0:1, :], gnb[0:1, :], start=True,
            stop=True), waits=[gn_tok, onesf_tok, cdv], inc="tt")
        gnb_tok = s.add("vec", lambda e: e.tensor_copy(gnb[:, :],
                                                       paux[:, 256:384]),
                        waits=[bc2], inc="ev")
        pt1 = s.add("vec", lambda e: e.tensor_tensor(
            rows[0:1, 768:1024], rows[0:1, 0:256], rows[0:1, 256:512],
            ALU.mult), waits=[r1, r2], inc="ev")
        pt2 = s.add("vec", lambda e: e.tensor_tensor(
            rows[0:1, 768:1024], rows[0:1, 768:1024], rows[0:1, 512:768],
            ALU.add), waits=[pt1, r3], inc="ev")
        bc3 = s.add("te", lambda e: e.matmul(
            paux[:, 0:256], onesf[0:1, :], rows[0:1, 768:1024], start=True,
            stop=True), waits=[pt2, cdv, gnb_tok], inc="tt")
        ptb_tok = s.add("vec", lambda e: e.tensor_copy(ptilb[:, :],
                                                       paux[:, 0:256]),
                        waits=[bc3], inc="ev")

        sini = s.add("vec", lambda e: e.memset(S_all[:, :], 0.0), inc="ev")
        prev_supd = sini
        for j in range(8):
            cj = s.dma("sy", lambda e, j=j: e.dma_start(
                out=cslab.ap().rearrange("d (h e) -> d h e", e=128),
                in_=C_all[j].rearrange("h d e -> d h e")),
                waits=[prev_supd])
            for h in range(H):
                tm = s.add("vec", lambda e, h=h, j=j: e.tensor_scalar(
                    tmpc[:, :], cslab[:, h * 128:(h + 1) * 128],
                    m_bc[:, j:j + 1], None, ALU.mult),
                    waits=[cj, mbc_tok], inc="ev")
                prev_supd = s.add("vec", lambda e, h=h, j=j:
                                  e.scalar_tensor_tensor(
                                      S_all[:, h * 128:(h + 1) * 128],
                                      S_all[:, h * 128:(h + 1) * 128],
                                      ptilb[:, j * 32 + h:j * 32 + h + 1],
                                      tmpc[:, :], ALU.mult, ALU.add),
                                  waits=[tm, ptb_tok, sini], inc="ev")
        sin_done = prev_supd

        prep_done = [sin_done, s.tok("ev"), s.tok("sc"), rope_done, dld,
                     kdld, vld, kld]
        tok_qtT_rd = None
        tok_kTt_rd = None
        tok_ptr_ev = [None, None]
        tok_pgb_rd = None
        tok_dec_rd = None
        tok_dm_rd = None
        tok_psc_rd = None
        tok_st_rd = None
        tok_xc_rd = None
        tok_py_rd = None
        tok_pz_rd = None
        tok_sc_rd = None
        tok_pcs_rd = None
        tok_kdec_rd = None
        tok_yfull_rd = None
        tok_y1_rd = None
        tok_selh_rd = None
        prev_selh_h = -1
        selh_tok = None
        for h in range(H):
            kv = h // 4
            selh_tok = s.dma("sy", lambda e, h=h: e.dma_start(
                out=selh_sb[:, :], in_=selh[:, h * 128:(h + 1) * 128]),
                waits=[tok_selh_rd])
            for c in range(NCH):
                col = c * 32 + h
                koff = c * 1024 + kv * 128
                qoff = c * DMODEL + h * 128
                t1 = s.add("te", lambda e, qoff=qoff: e.transpose(
                    ptr[0][:, 0:128], q_sb[:, qoff:qoff + 128],
                    identb[:, :]),
                    waits=prep_done + [tok_ptr_ev[0], id_tok], inc="tt")
                v2 = s.add("vec", lambda e: e.tensor_copy(
                    qtT[:, :], ptr[0][:, 0:128]),
                    waits=[t1, tok_qtT_rd], inc="ev")
                tok_ptr_ev[0] = v2
                t1b = s.add("te", lambda e, koff=koff: e.transpose(
                    ptr[1][:, 0:128], k_sb[:, koff:koff + 128],
                    identb[:, :]),
                    waits=[kld, id_tok, tok_ptr_ev[1]], inc="tt")
                v2b = s.add("vec", lambda e: e.tensor_copy(
                    kTt[:, :], ptr[1][:, 0:128]),
                    waits=[t1b, tok_kTt_rd], inc="ev")
                tok_ptr_ev[1] = v2b
                t2 = s.add("te", lambda e, c=c: e.matmul(
                    pgb[:, 0:128], selh_sb[:, :],
                    gts[0:32, c * 128:(c + 1) * 128], start=True, stop=True),
                    waits=[selh_tok, gtld, tok_pgb_rd], inc="tt")
                tok_selh_rd = t2
                v0 = s.add("vec", lambda e, col=col: e.tensor_scalar(
                    dec0[:, :], pgb[:, 0:128], negG[:, col:col + 1],
                    0.0, ALU.add, ALU.min),
                    waits=[t2, ng_tok, tok_dec_rd], inc="ev")
                tok_pgb_rd = v0
                s1 = s.add("sc", lambda e: e.activation(
                    dec_sb[:, :], dec0[:, :], AF.Exp), waits=[v0, tok_dm_rd],
                    inc="sc")
                v3 = s.add("vec", lambda e: e.tensor_tensor(
                    dm[:, :], dec_sb[:, :], tri_sb[:, :], ALU.mult),
                    waits=[s1, tri_tok, tok_dm_rd], inc="ev")
                tok_dec_rd = v3
                t3 = s.add("te", lambda e: e.matmul(
                    psc[:, 0:128], kTt[:, :], qtT[:, :],
                    start=True, stop=True),
                    waits=[v2, v2b, tok_psc_rd], inc="tt")
                tok_kTt_rd = t3
                v4 = s.add("vec", lambda e: e.tensor_tensor(
                    st_sb[:, :], dm[:, :], psc[:, 0:128], ALU.mult),
                    waits=[v3, t3, tok_st_rd], inc="ev")
                tok_dm_rd = v4
                tok_psc_rd = v4
                v5 = s.add("vec", lambda e, koff=koff, col=col: e.tensor_scalar(
                    xc_sb[:, :], v_sb[:, koff:koff + 128],
                    dt_sb[:, col:col + 1], None, ALU.mult),
                    waits=[tok_xc_rd], inc="ev")
                t4 = s.add("te", lambda e: e.matmul(
                    py[:, 0:128], st_sb[:, :], xc_sb[:, :], start=True,
                    stop=True), waits=[v4, v5, tok_py_rd], inc="tt")
                tok_st_rd = t4
                s2 = s.add("sc", lambda e, h=h: e.activation(
                    sc_sb[:, :], S_all[:, h * 128:(h + 1) * 128], AF.Copy,
                    bias=0.0), waits=[tok_sc_rd, s.tok("ev")], inc="sc")
                t5 = s.add("te", lambda e: e.matmul(
                    pz[:, 0:128], qtT[:, :], sc_sb[:, :], start=True,
                    stop=True), waits=[s2, v2, tok_pz_rd], inc="tt")
                tok_qtT_rd = t5
                tok_sc_rd = t5
                g1 = s.add("gp", lambda e, koff=koff, col=col: e.tensor_scalar(
                    kdec[:, :], k_sb[:, koff:koff + 128],
                    kdf[:, col:col + 1], None, ALU.mult),
                    waits=[kld, kdld, tok_kdec_rd], inc="gc")
                t6 = s.add("te", lambda e: e.matmul(
                    pcs[:, 0:128], kdec[:, :], xc_sb[:, :], start=True,
                    stop=True), waits=[g1, v5, tok_pcs_rd], inc="tt")
                tok_kdec_rd = t6
                tok_xc_rd = t6
                v7 = s.add("vec", lambda e, h=h, col=col:
                           e.scalar_tensor_tensor(
                               S_all[:, h * 128:(h + 1) * 128],
                               S_all[:, h * 128:(h + 1) * 128],
                               cdb[:, col:col + 1], pcs[:, 0:128],
                               ALU.mult, ALU.add),
                           waits=[t6, s2, cde], inc="ev")
                tok_pcs_rd = v7
                v6a = s.add("vec", lambda e: e.tensor_copy(
                    zc_sb[:, :], pz[:, 0:128]),
                    waits=[t5, tok_pz_rd], inc="ev")
                v6 = s.add("vec", lambda e, col=col: e.scalar_tensor_tensor(
                    yfull[:, :], zc_sb[:, :], expG[:, col:col + 1],
                    py[:, 0:128], ALU.mult, ALU.add),
                    waits=[t4, v6a, eg_tok, tok_yfull_rd], inc="ev")
                tok_py_rd = v6
                tok_pz_rd = v6a
                s3 = s.add("sc", lambda e: e.activation(
                    y1_sb[:, :], yfull[:, :], AF.Square, accum_out=ss[:, :]),
                    waits=[v6, tok_y1_rd], inc="sc")
                s4 = s.add("sc", lambda e: e.activation(
                    rr[:, :], ss[:, :], AF.Ln, scale=np.float32(1.0 / 128.0),
                    bias=np.float32(RMS_EPS)), waits=[s3], inc="sc")
                s5 = s.add("sc", lambda e: e.activation(
                    rr[:, :], rr[:, :], AF.Exp, scale=np.float32(-0.5)),
                    waits=[s4], inc="sc")
                v8 = s.add("vec", lambda e: e.tensor_scalar(
                    y1_sb[:, :], yfull[:, :], rr[:, :], None, ALU.mult),
                    waits=[s5, s3], inc="ev")
                tok_yfull_rd = v8
                v9 = s.add("vec", lambda e, qoff=qoff: e.tensor_tensor(
                    q_sb[:, qoff:qoff + 128], y1_sb[:, :], gnb[:, :],
                    ALU.mult), waits=[v8, gnb_tok, t5], inc="ev")
                tok_y1_rd = v9
                if dbg and (h, c) in ((0, 0), (0, 1)):
                    base = 0 if c == 0 else 12 * 128
                    def dcp(slot, buf, wt):
                        tok = s.add("vec",
                                    lambda e, slot=slot, buf=buf, base=base:
                                    e.tensor_copy(
                                        dbg_sb[:, base + slot * 128:
                                               base + (slot + 1) * 128],
                                        buf), waits=wt, inc="ev")
                        return tok
                    dcp(0, qtT[:, :], [v2])
                    dcp(1, kTt[:, :], [v2b])
                    dcp(2, dec_sb[:, :], [s1])
                    dcp(3, dm[:, :], [v3])
                    w = dcp(4, psc[:, 0:128], [t3])
                    tok_psc_rd = w
                    dcp(5, st_sb[:, :], [v4])
                    dcp(6, xc_sb[:, :], [v5])
                    dcp(7, yfull[:, :], [v6])
                    w = dcp(8, py[:, 0:128], [t4])
                    tok_py_rd = w
                    dcp(9, zc_sb[:, :], [v6a])
                    dcp(10, S_all[:, 0:128], [v7])
                    w = dcp(11, pcs[:, 0:128], [t6])
                    tok_pcs_rd = w
        s.dma("sy", lambda e: e.dma_start(
            out=yr_h.ap().rearrange("(t p) o -> p t o", p=128),
            in_=q_sb.ap().rearrange("p (t o) -> p t o", o=DMODEL)),
            waits=[s.tok("ev")])
        if dbg:
            s.dma("sy", lambda e: e.dma_start(out=dbg_h[:, :],
                                              in_=dbg_sb[:, :]),
                  waits=[s.tok("ev")])

    s.barrier()


def emit_mamba_c(s, nc, io):
    x = io["x"]
    xnT_h = io["xnT_h"]
    yr_h = io["yr_h"]
    wgT = io["wgT"]
    woT = io["woT"]
    ident = io["ident"]
    xo = io["xo"]
    with contextlib.ExitStack() as stk:
        _p = _stage_pfx()

        def sb(name, shape, dt):
            return stk.enter_context(nc.sbuf_tensor(_p + name, shape, dt))

        def ps(name, shape, dt):
            return stk.enter_context(nc.psum_tensor(_p + name, shape, dt))

        yh_sb = sb("yh_sb", [128, NT * DMODEL], BF16)
        identb = sb("identb", [128, 128], BF16)
        yrt0 = sb("yrt0", [128, 512], BF16)
        yrt1 = sb("yrt1", [128, 512], BF16)
        sg0 = sb("sg0", [128, 512], BF16)
        sg1 = sb("sg1", [128, 512], BF16)
        xold0 = sb("xold0", [128, 512], F32)
        xold1 = sb("xold1", [128, 512], F32)
        stage0 = sb("stage0", [128, 512], F32)
        stage1 = sb("stage1", [128, 512], F32)
        pmm = [ps("pm0", [128, 512], F32), ps("pm1", [128, 512], F32),
               ps("pm2", [128, 512], F32), ps("pm3", [128, 512], F32)]
        ptr = [ps("ptr0", [128, 1024], BF16), ps("ptr1", [128, 1024], BF16)]
        id_tok = s.dma("gp", lambda e: e.dma_start(out=identb[:, :],
                                                   in_=ident[:, :]))

        cm_slab = nc.sbuf_tensor(_p + "slab", [128, NI * 512], BF16)
        slab = cm_slab.__enter__()
        cm_xnT = nc.sbuf_tensor(_p + "xnT", [128, NI * T], BF16)
        xnT = cm_xnT.__enter__()
        xn_tok = s.dma("sy", lambda e: e.dma_start(
            out=xnT.ap().rearrange("p (i t) -> p i t", t=T),
            in_=xnT_h.ap().rearrange("(i p) t -> p i t", p=128)))
        evac_tok_bank = [None] * 4
        slab_free = None
        yr_rd = [None, None]
        sg_rd = [None, None]
        for oc in range(8):
            tk = s.dma("gp", lambda e, oc=oc: e.dma_start(
                out=slab.ap().rearrange("p (i o) -> p i o", o=512),
                in_=wgT[:, oc * 512:(oc + 1) * 512]
                .rearrange("(i p) o -> p i o", p=128)), waits=[slab_free])
            for t in range(NT):
                it = oc * NT + t
                bank = t % 4
                p = it % 2
                s.add("te", lambda e, t=t, bank=bank: e.matmul(
                    pmm[bank][:, :],
                    xnT[:, 0 * T + t * 128: 0 * T + (t + 1) * 128],
                    slab[:, 0:512], start=True, stop=False),
                    waits=[xn_tok, tk, evac_tok_bank[bank]])
                for i in range(1, NI):
                    s.add("te", lambda e, i=i, t=t, bank=bank: e.matmul(
                        pmm[bank][:, :],
                        xnT[:, i * T + t * 128: i * T + (t + 1) * 128],
                        slab[:, i * 512:(i + 1) * 512],
                        start=False, stop=(i == NI - 1)),
                        inc="mm" if i == NI - 1 else None)
                mm_tok = s.tok("mm")
                ld = s.dma("sy", lambda e, t=t, oc=oc, p=p: e.dma_start(
                    out=[yrt0, yrt1][p][:, :],
                    in_=yr_h[t * 128:(t + 1) * 128,
                             oc * 512:(oc + 1) * 512]), waits=[yr_rd[p]])
                sg = s.add("sc", lambda e, bank=bank, p=p: e.activation(
                    [sg0, sg1][p][:, :], pmm[bank][:, :], AF.Silu),
                    waits=[mm_tok, sg_rd[p]], inc="sc")
                ev = s.add("vec", lambda e, t=t, oc=oc, p=p: e.tensor_tensor(
                    yh_sb[:, t * DMODEL + oc * 512: t * DMODEL + oc * 512 + 512],
                    [sg0, sg1][p][:, :], [yrt0, yrt1][p][:, :], ALU.mult),
                    waits=[sg, ld], inc="ev")
                yr_rd[p] = ev
                sg_rd[p] = ev
                evac_tok_bank[bank] = s.add(
                    "vec", lambda e: e.engine_nop(), waits=[ev], inc="ev")                     if False else ev
            slab_free = s.tok("mm")
        gate_done = s.tok("ev")
        gmm_done = s.tok("mm")
        cm_xnT.__exit__(None, None, None)
        cm_slab.__exit__(None, None, None)

        yhT = sb("yhT", [128, NI * T], BF16)
        slab2 = sb("slab2", [128, NI * 512], BF16)
        prev_ptr = [None, None]
        evs = []
        for n in range(NT * NI):
            t, i = divmod(n, NI)
            pb = n % 2
            tr = s.add("te", lambda e, t=t, i=i, pb=pb: e.transpose(
                ptr[pb][:, 0:128],
                yh_sb[:, t * DMODEL + i * 128: t * DMODEL + (i + 1) * 128],
                identb[:, :]),
                waits=[gate_done, id_tok, prev_ptr[pb]], inc="tt")
            ev = s.add("vec", lambda e, t=t, i=i, pb=pb: e.tensor_copy(
                yhT[:, i * T + t * 128: i * T + (t + 1) * 128],
                ptr[pb][:, 0:128]), waits=[tr], inc="ev")
            prev_ptr[pb] = ev
            evs.append(ev)
        yhT_done = s.tok("ev")

        evac_tok_bank = [None] * 4
        slab_free = None
        store_tok = [None, None]
        evac_read_xold = [None, None]
        for oc in range(8):
            tk = s.dma("gp", lambda e, oc=oc: e.dma_start(
                out=slab2.ap().rearrange("p (i o) -> p i o", o=512),
                in_=woT[:, oc * 512:(oc + 1) * 512]
                .rearrange("(i p) o -> p i o", p=128)),
                waits=[slab_free if slab_free is not None else gmm_done])
            for t in range(NT):
                it = oc * NT + t
                bank = t % 4
                p = it % 2
                s.add("te", lambda e, t=t, bank=bank: e.matmul(
                    pmm[bank][:, :],
                    yhT[:, 0 * T + t * 128: 0 * T + (t + 1) * 128],
                    slab2[:, 0:512], start=True, stop=False),
                    waits=[yhT_done, tk, evac_tok_bank[bank]])
                for i in range(1, NI):
                    s.add("te", lambda e, i=i, t=t, bank=bank: e.matmul(
                        pmm[bank][:, :],
                        yhT[:, i * T + t * 128: i * T + (t + 1) * 128],
                        slab2[:, i * 512:(i + 1) * 512],
                        start=False, stop=(i == NI - 1)),
                        inc="mm" if i == NI - 1 else None)
                mm_tok = s.tok("mm")
                xl = s.dma("sy", lambda e, oc=oc, t=t, p=p: e.dma_start(
                    out=[xold0, xold1][p][:, :],
                    in_=x[t * 128:(t + 1) * 128, oc * 512:(oc + 1) * 512]),
                    waits=[evac_read_xold[p]])
                ev = s.add("vec", lambda e, p=p, bank=bank:
                           e.scalar_tensor_tensor(
                               [stage0, stage1][p][:, :], pmm[bank][:, :],
                               1.0, [xold0, xold1][p][:, :], ALU.mult,
                               ALU.add),
                           waits=[mm_tok, xl, store_tok[p]], inc="ev")
                evac_tok_bank[bank] = ev
                evac_read_xold[p] = ev
                store_tok[p] = s.dma("sy", lambda e, oc=oc, t=t, p=p:
                                     e.dma_start(
                                         out=xo[t * 128:(t + 1) * 128,
                                                oc * 512:(oc + 1) * 512],
                                         in_=[stage0, stage1][p][:, :]),
                                     waits=[ev])
            slab_free = s.tok("mm")

    s.barrier()


def build_launch_A():
    nc = bass.Bass("TRN2", target_bir_lowering=False)
    E_IN, E_OUT = "ExternalInput", "ExternalOutput"
    io = {}
    io["ident"] = nc.dram_tensor("ident", [128, 128], F32, kind=E_IN)
    io["ones_r"] = nc.dram_tensor("ones_r", [1, 128], F32, kind=E_IN)
    x0 = nc.dram_tensor("x0", [T, DMODEL], F32, kind=E_IN)
    w0T = nc.dram_tensor("w0T", [DMODEL, DMODEL], F32, kind=E_IN)
    scinv = nc.dram_tensor("scinv", [1, 1024], F32, kind=E_IN)
    b0 = nc.dram_tensor("b0", [1, DMODEL], F32, kind=E_IN)
    x1 = nc.dram_tensor("x1", [T, DMODEL], F32, kind=E_OUT)
    ma = {
        "wkT": nc.dram_tensor("wkT", [DMODEL, 1024], F32, kind=E_IN),
        "wvT": nc.dram_tensor("wvT", [DMODEL, 1024], F32, kind=E_IN),
        "winT": nc.dram_tensor("winT", [DMODEL, 32], F32, kind=E_IN),
        "dtb": nc.dram_tensor("dtb", [1, 32], F32, kind=E_IN),
        "alog": nc.dram_tensor("alog", [1, 32], F32, kind=E_IN),
        "cosT": nc.dram_tensor("cosT", [T, 128], F32, kind=E_IN),
        "sinLT": nc.dram_tensor("sinLT", [T, 128], F32, kind=E_IN),
        "xnT_h": nc.dram_tensor("xnT_h", [DMODEL, T], BF16, kind=E_OUT),
        "k_h": nc.dram_tensor("k_h", [T, 1024], BF16, kind=E_OUT),
        "v_h": nc.dram_tensor("v_h", [T, 1024], BF16, kind=E_OUT),
        "dt_h": nc.dram_tensor("dt_h", [T, 32], F32, kind=E_OUT),
        "G_h": nc.dram_tensor("G_h", [T, 32], F32, kind=E_OUT),
        "gt_h": nc.dram_tensor("gt_h", [NCH * 32, 128], F32, kind=E_OUT),
        "gtot_h": nc.dram_tensor("gtot_h", [1, 256], F32, kind=E_OUT),
        "kdf_h": nc.dram_tensor("kdf_h", [T, 32], F32, kind=E_OUT),
        "C_h": nc.dram_tensor("C_h", [32, 128, 128], F32, kind=E_OUT),
        "P_h": nc.dram_tensor("P_h", [1, 32], F32, kind=E_OUT),
    }
    mp = {
        "masksl": nc.dram_tensor("masksl", [512, DMODEL], I32, kind=E_IN),
        "wTsl": nc.dram_tensor("wTsl", [DMODEL, 512], F32, kind=E_IN),
        "sel32": nc.dram_tensor("sel32", [128, 4], F32, kind=E_IN),
        "rep4": nc.dram_tensor("rep4", [4, 128], F32, kind=E_IN),
        "wmT": nc.dram_tensor("wmT", [DMODEL, 512], BF16, kind=E_OUT),
    }

    s = Sched()
    emit_ffn(s, nc, dict(io, x=x0, wT=w0T, scinv=scinv, bias=b0, xo=x1),
             "ffn0")
    emit_mamba_a(s, nc, dict(io, x=x1, **ma))
    emit_maskprep(s, nc, dict(io, **mp))
    s.emit(nc)
    return nc


def build_launch_B():
    nc = bass.Bass("TRN2", target_bir_lowering=False)
    E_IN, E_OUT = "ExternalInput", "ExternalOutput"
    io = {}
    io["ident"] = nc.dram_tensor("ident", [128, 128], F32, kind=E_IN)
    io["ones_r"] = nc.dram_tensor("ones_r", [1, 128], F32, kind=E_IN)
    x1 = nc.dram_tensor("x1", [T, DMODEL], F32, kind=E_IN)
    mb = {
        "xnT_h": nc.dram_tensor("xnT_h", [DMODEL, T], BF16, kind=E_IN),
        "wqT": nc.dram_tensor("wqT", [DMODEL, DMODEL], F32, kind=E_IN),
        "k_h": nc.dram_tensor("k_h", [T, 1024], BF16, kind=E_IN),
        "v_h": nc.dram_tensor("v_h", [T, 1024], BF16, kind=E_IN),
        "dt_h": nc.dram_tensor("dt_h", [T, 32], F32, kind=E_IN),
        "G_h": nc.dram_tensor("G_h", [T, 32], F32, kind=E_IN),
        "gt_h": nc.dram_tensor("gt_h", [NCH * 32, 128], F32, kind=E_IN),
        "gtot_h": nc.dram_tensor("gtot_h", [1, 256], F32, kind=E_IN),
        "kdf_h": nc.dram_tensor("kdf_h", [T, 32], F32, kind=E_IN),
        "C_all": nc.dram_tensor("C_all", [8, 32, 128, 128], F32, kind=E_IN),
        "P_row": nc.dram_tensor("P_row", [1, 256], F32, kind=E_IN),
        "m_row": nc.dram_tensor("m_row", [1, 256], F32, kind=E_IN),
        "invm_row": nc.dram_tensor("invm_row", [1, 256], F32, kind=E_IN),
        "m_bcd": nc.dram_tensor("m_bcd", [128, 8], F32, kind=E_IN),
        "selh": nc.dram_tensor("selh", [32, 4096], F32, kind=E_IN),
        "tri": nc.dram_tensor("tri", [128, 128], F32, kind=E_IN),
        "gnorm": nc.dram_tensor("gnorm", [1, 128], F32, kind=E_IN),
        "cosT": nc.dram_tensor("cosT", [T, 128], F32, kind=E_IN),
        "sinLT": nc.dram_tensor("sinLT", [T, 128], F32, kind=E_IN),
    }
    yr_h = nc.dram_tensor("yr_h", [T, DMODEL], BF16)
    wgT = nc.dram_tensor("wgT", [DMODEL, DMODEL], F32, kind=E_IN)
    woT = nc.dram_tensor("woT", [DMODEL, DMODEL], F32, kind=E_IN)
    w0T = nc.dram_tensor("w0T", [DMODEL, DMODEL], F32, kind=E_IN)
    scinv = nc.dram_tensor("scinv", [1, 1024], F32, kind=E_IN)
    b0 = nc.dram_tensor("b0", [1, DMODEL], F32, kind=E_IN)
    qwT = nc.dram_tensor("qwT", [DMODEL, DMODEL], I32, kind=E_IN)
    qscale = nc.dram_tensor("qscale", [1, DMODEL], F32, kind=E_IN)
    b1 = nc.dram_tensor("b1", [1, DMODEL], F32, kind=E_IN)
    loraAT = nc.dram_tensor("loraAT", [DMODEL, 16], F32, kind=E_IN)
    loraBT = nc.dram_tensor("loraBT", [16, DMODEL], F32, kind=E_IN)
    wmT = nc.dram_tensor("wmT", [DMODEL, DMODEL], BF16, kind=E_IN)
    bm = nc.dram_tensor("bm", [1, DMODEL], F32, kind=E_IN)
    x2 = nc.dram_tensor("x2", [T, DMODEL], F32)
    x3 = nc.dram_tensor("x3", [T, DMODEL], F32)
    x4 = nc.dram_tensor("x4", [T, DMODEL], F32)
    x5 = nc.dram_tensor("x5", [T, DMODEL], F32)
    out = nc.dram_tensor("out", [T, DMODEL], F32, kind=E_OUT)

    s = Sched()
    emit_mamba_b(s, nc, dict(io, yr_h=yr_h, **mb))
    emit_mamba_c(s, nc, dict(io, x=x1, xnT_h=mb["xnT_h"], yr_h=yr_h,
                             wgT=wgT, woT=woT, xo=x2))
    emit_ffn(s, nc, dict(io, x=x2, wT=w0T, scinv=scinv, bias=b0, xo=x3),
             "ffn0")
    emit_ffn(s, nc, dict(io, x=x3, wT=qwT, qscale=qscale, bias=b1,
                         loraAT=loraAT, loraBT=loraBT, xo=x4), "ffn1")
    emit_ffn(s, nc, dict(io, x=x4, wT=wmT, bias=bm, xo=x5), "maskmm")
    emit_ffn(s, nc, dict(io, x=x5, wT=qwT, qscale=qscale, bias=b1,
                         loraAT=loraAT, loraBT=loraBT, xo=out), "ffn1")
    s.emit(nc)
    return nc


def kernel(**inputs):
    c = _prep_common(inputs)
    x0 = np.ascontiguousarray(
        inputs["L0_attn_hidden_states"].reshape(SEQ, DMODEL))
    w0T = np.ascontiguousarray(inputs["L0_ffn_weight"].T)
    scinv = np.ascontiguousarray(
        inputs["L0_ffn_weight_scale_inv"].T).reshape(1, 1024)
    b0 = np.ascontiguousarray(inputs["L0_ffn_bias"].reshape(1, DMODEL))
    wkT = np.ascontiguousarray(inputs["L0_attn_Wk"].T)
    wvT = np.ascontiguousarray(inputs["L0_attn_Wv"].T)
    winT = np.ascontiguousarray(inputs["L0_attn_Win"].T)
    wTm = np.ascontiguousarray(inputs["L1_attn_weight"].T)
    mask = inputs["L1_attn_mask"]

    nc_a = build_launch_A()
    ins_a = []
    for r in range(NCORES):
        sl = slice(r * T, (r + 1) * T)
        ins_a.append({
            "ident": c["ident"], "ones_r": c["ones_r"],
            "x0": np.ascontiguousarray(x0[sl]),
            "w0T": w0T, "scinv": scinv, "b0": b0,
            "wkT": wkT, "wvT": wvT, "winT": winT,
            "dtb": np.ascontiguousarray(
                inputs["L0_attn_dt_bias"].reshape(1, 32)),
            "alog": np.ascontiguousarray(
                inputs["L0_attn_A_log_bias"].reshape(1, 32)),
            "cosT": np.ascontiguousarray(c["cos"][sl]),
            "sinLT": np.ascontiguousarray(c["sinL"][sl]),
            "masksl": np.ascontiguousarray(mask[r * 512:(r + 1) * 512]),
            "wTsl": np.ascontiguousarray(wTm[:, r * 512:(r + 1) * 512]),
            "sel32": c["sel32"], "rep4": c["rep4"],
        })
    res_a = _run(nc_a, ins_a)

    C_all = np.ascontiguousarray(np.stack([r["C_h"] for r in res_a], 0))
    P_row = np.ascontiguousarray(
        np.stack([r["P_h"][0] for r in res_a], 0).reshape(1, 256))
    wmT_full = np.ascontiguousarray(
        np.concatenate([r["wmT"] for r in res_a], 1))

    nc_b = build_launch_B()
    qwT = np.ascontiguousarray(inputs["L1_ffn_qweight"].T)
    qsc = np.ascontiguousarray(inputs["L1_ffn_qscale"].reshape(1, DMODEL))
    b1 = np.ascontiguousarray(inputs["L1_ffn_bias"].reshape(1, DMODEL))
    lAT = np.ascontiguousarray(inputs["L1_ffn_lora_A"].T)
    lBT = np.ascontiguousarray(inputs["L1_ffn_lora_B"].T)
    wqT = np.ascontiguousarray(inputs["L0_attn_Wq"].T)
    wgT = np.ascontiguousarray(inputs["L0_attn_Wg"].T)
    woT = np.ascontiguousarray(inputs["L0_attn_Wo"].T)
    bmm = np.ascontiguousarray(inputs["L1_attn_bias"].reshape(1, DMODEL))
    ins_b = []
    for r in range(NCORES):
        sl = slice(r * T, (r + 1) * T)
        mrow = np.zeros((1, 256), dtype=np.float32)
        mrow[0, :r * 32] = 1.0
        mbc = np.zeros((128, 8), dtype=np.float32)
        mbc[:, :r] = 1.0
        o = res_a[r]
        ins_b.append({
            "ident": c["ident"], "ones_r": c["ones_r"],
            "x1": o["x1"], "xnT_h": o["xnT_h"], "wqT": wqT,
            "k_h": o["k_h"], "v_h": o["v_h"], "dt_h": o["dt_h"],
            "G_h": o["G_h"], "gt_h": o["gt_h"], "gtot_h": o["gtot_h"],
            "kdf_h": o["kdf_h"], "C_all": C_all, "P_row": P_row,
            "m_row": mrow, "invm_row": np.ascontiguousarray(1.0 - mrow),
            "m_bcd": mbc, "selh": c["selh"], "tri": c["tri"],
            "gnorm": np.ascontiguousarray(
                inputs["L0_attn_gnorm_w"].reshape(1, 128)),
            "cosT": np.ascontiguousarray(c["cos"][sl]),
            "sinLT": np.ascontiguousarray(c["sinL"][sl]),
            "wgT": wgT, "woT": woT, "w0T": w0T, "scinv": scinv, "b0": b0,
            "qwT": qwT, "qscale": qsc, "b1": b1, "loraAT": lAT,
            "loraBT": lBT, "wmT": wmT_full, "bm": bmm,
        })
    res_b = _run(nc_b, ins_b)
    out = np.concatenate([r["out"] for r in res_b], 0)
    return out.reshape(1, SEQ, DMODEL).astype(np.float32)
